# version 07
# baseline (speedup 1.0000x reference)
import math
import numpy as np
import concourse.bass as bass
import concourse.mybir as mybir
from concourse.bass_utils import run_bass_kernel_spmd
from contextlib import ExitStack

F32 = mybir.dt.float32
BF16 = mybir.dt.bfloat16
AF = mybir.ActivationFunctionType
ALU = mybir.AluOpType
AX = mybir.AxisListType
F16 = mybir.dt.float16

DEBUG = False
BUILD_PHASES = 99
P1A_GROUPS = 8
P1A_OWN = True

D = 4096
NF = 32
NH = 32
HD = 128
S = 4096
NKC = 32
T = 1024
NT = 8
NEG = -1e30
C0 = 3968
WD = 8064
LN_EPS = 1e-5
RMS_EPS = 1.1920929e-07
TAU_INV = 10.0
SINK_ITERS = 50

SLOPES = [2.0 ** (-0.25 * (h + 1)) for h in range(NH)]
NW = [min(NKC, math.ceil((208.0 / s) / 128.0) + 5) for s in SLOPES]


def _i_range(kc, nw):
    i_lo = max(0, -(-(kc - 3) // 4))
    i_hi = min(NT - 1, (kc + nw - 4) // 4)
    return i_lo, i_hi


def _start_kc(i, nw):
    return max(0, 4 * i + 4 - nw)


class Eng:
    def __init__(self, nc, handle, name, stack):
        self.h = handle
        self.name = name
        self.sem = stack.enter_context(nc.semaphore("s" + name))
        self.n = 0
        self.last = None
        self.marked = True

    def op(self, inst):
        self.last = inst
        self.marked = False
        return inst

    def mark(self):
        if not self.marked:
            self.last.then_inc(self.sem, 1)
            self.n += 1
            self.marked = True
        return self.n

    def wait(self, other, tok):
        if tok > 0:
            self.h.wait_ge(other.sem, tok)


class Dma:

    def __init__(self, nc, qh, name, stack, nslots=2):
        self.qh = qh
        self.sems = [stack.enter_context(nc.semaphore(f"d{name}{i}"))
                     for i in range(nslots)]
        self.cnt = [0] * nslots
        self.nslots = nslots

    def start(self, slot, out_ap, in_ap):
        s = slot % self.nslots
        self.qh.dma_start(out_ap, in_ap).then_inc(self.sems[s], 16)
        self.cnt[s] += 16
        return (s, self.cnt[s])

    def wait(self, eng, tok):
        s, v = tok
        eng.h.wait_ge(self.sems[s], v)


def build_program():
    nc = bass.Bass("TRN2", target_bir_lowering=False)

    xk = nc.dram_tensor("xk", [S, D], F32, kind="ExternalInput").ap()
    xq = nc.dram_tensor("xq", [T, D], F32, kind="ExternalInput").ap()
    wqkv = nc.dram_tensor("wqkv", [D, 4352], F32, kind="ExternalInput").ap()
    wo = nc.dram_tensor("wo", [D, D], F32, kind="ExternalInput").ap()
    wbt = nc.dram_tensor("wbt", [D, D], F32, kind="ExternalInput").ap()
    biasb = nc.dram_tensor("biasb", [1, D], F32, kind="ExternalInput").ap()
    wrppt = nc.dram_tensor("wrppt", [D, 24], F32, kind="ExternalInput").ap()
    basespp = nc.dram_tensor("basespp", [128, 24], F32, kind="ExternalInput").ap()
    gate = nc.dram_tensor("gate", [1, 1], F32, kind="ExternalInput").ap()
    wbr = nc.dram_tensor("wbr", [1024, 1024], F32, kind="ExternalInput").ap()
    dmask = nc.dram_tensor("dmask", [128, WD], F32, kind="ExternalInput").ap()
    y = nc.dram_tensor("y", [T, D], F32, kind="ExternalOutput").ap()
    wo_bf_d = nc.dram_tensor("wo_bf_d", [D, D], BF16, kind="Internal").ap()
    wbt_bf_d = nc.dram_tensor("wbt_bf_d", [D, D], BF16, kind="Internal").ap()
    wq_bf_d = nc.dram_tensor("wq_bf_d", [D, D], BF16, kind="Internal").ap()
    taps = {}
    if DEBUG:
        taps["t_kfm"] = nc.dram_tensor("t_kfm", [128, S], F32, kind="ExternalOutput").ap()
        taps["t_v"] = nc.dram_tensor("t_v", [128, NKC, 129], F32, kind="ExternalOutput").ap()
        taps["t_lnT"] = nc.dram_tensor("t_lnT", [128, NF, T], F32, kind="ExternalOutput").ap()
        taps["t_attn"] = nc.dram_tensor("t_attn", [128, NT, D], F32, kind="ExternalOutput").ap()
        taps["t_x1"] = nc.dram_tensor("t_x1", [128, NT, D], F32, kind="ExternalOutput").ap()
        taps["t_x2"] = nc.dram_tensor("t_x2", [128, NT, D], F32, kind="ExternalOutput").ap()
        taps["t_x3"] = nc.dram_tensor("t_x3", [128, NT, D], F32, kind="ExternalOutput").ap()

    top = ExitStack()
    sb = lambda nm, shp, dt=F32: top.enter_context(nc.sbuf_tensor(nm, shp, dt))

    PE = Eng(nc, nc.tensor, "PE", top)
    VE = Eng(nc, nc.vector, "VE", top)
    SC = Eng(nc, nc.scalar, "SC", top)
    PL = Eng(nc, nc.gpsimd, "PL", top)
    SP = Eng(nc, nc.sync, "SP", top)
    ALL = [PE, VE, SC, PL, SP]

    dXK = Dma(nc, nc.sync, "xk", top)
    dXQ = Dma(nc, nc.sync, "xq", top)
    dCN = Dma(nc, nc.sync, "cn", top, nslots=1)
    dST = Dma(nc, nc.sync, "st", top)
    gW = Dma(nc, nc.gpsimd, "gw", top)
    sWQ = Dma(nc, nc.sync, "wq", top)
    gWO = Dma(nc, nc.sync, "go", top)
    gB = Dma(nc, nc.gpsimd, "gb", top)
    gPC = Dma(nc, nc.gpsimd, "pc", top, nslots=4)

    def barrier():
        toks = [(e, e.mark()) for e in (PE, VE, SC, PL)]
        for e in ALL:
            for o, t in toks:
                if o is not e and t > 0:
                    e.h.wait_ge(o.sem, t)

    ident = sb("ident", [128, 128], BF16)
    bspp = sb("bspp", [128, 24])
    wrpp_bf = sb("wrpp_bf", [128, NF, 24], BF16)
    gcol = sb("gcol", [128, 1])
    g1col = sb("g1col", [128, 1])
    onesb = sb("onesb", [1, 128], BF16)
    onesf = sb("onesf", [1, 128])
    eps1 = sb("eps1", [128, 1])
    epsr = sb("epsr", [128, 1])
    gtmp = sb("gtmp", [1, 1])

    blk = top.enter_context(nc.Block())

    def emit():
        p1 = ExitStack()
        p1a = ExitStack()
        sb1 = lambda nm, shp, dt=F32: p1.enter_context(nc.sbuf_tensor(nm, shp, dt))
        sb1a = lambda nm, shp, dt=F32: p1a.enter_context(nc.sbuf_tensor(nm, shp, dt))
        lnT_own = sb1("lnT_own", [128, NF, T], BF16)
        kfm = sb1("kfm", [128, S], BF16)
        vtiles = sb1("vtiles", [128, NKC, 129], BF16)
        pa = ExitStack()


        def tap_quiesce():
            for dmaq in (dST, gWO, gPC):
                for s_, c_ in zip(dmaq.sems, dmaq.cnt):
                    if c_:
                        for e_ in ALL:
                            e_.h.wait_ge(s_, c_)

        PL.op(nc.gpsimd.memset(ident[:], 1.0))
        t_ms = PL.mark()
        PL.h.wait_ge(PL.sem, t_ms)
        PL.op(nc.gpsimd.affine_select(out=ident[:], in_=ident[:],
                                      compare_op=ALU.is_equal, fill=0.0, base=0,
                                      pattern=[[-1, 128]], channel_multiplier=1))
        PL.op(nc.gpsimd.memset(onesb[:], 1.0))
        PL.op(nc.gpsimd.memset(onesf[:], 1.0))
        PL.op(nc.gpsimd.memset(eps1[:], LN_EPS))
        PL.op(nc.gpsimd.memset(epsr[:], RMS_EPS))
        PL.op(nc.gpsimd.memset(vtiles[:, :, 128], 1.0))
        t_p0pl = PL.mark()

        dCN.start(0, bspp[:], basespp[:])
        tk_cn = dCN.start(0, gtmp[:], gate[:])

        wkv_bf = sb1a("wkv_bf", [128, NF, 256], BF16)
        gW.start(0, wkv_bf[:], wqkv[:, 4096:4352].rearrange("(c p) n -> p c n", p=128))
        tk_w0 = gW.start(0, wrpp_bf[:], wrppt[:].rearrange("(c p) n -> p c n", p=128))
        for sl4 in range(4):
            gPC.start(0, wq_bf_d[1024 * sl4:1024 * (sl4 + 1), :],
                      wqkv[1024 * sl4:1024 * (sl4 + 1), 0:4096])
        for sl4 in range(4):
            gPC.start(1, wo_bf_d[1024 * sl4:1024 * (sl4 + 1), :],
                      wo[1024 * sl4:1024 * (sl4 + 1), :])
        for sl4 in range(4):
            gPC.start(2, wbt_bf_d[1024 * sl4:1024 * (sl4 + 1), :],
                      wbt[1024 * sl4:1024 * (sl4 + 1), :])
        tk_pc_q = (0, 4 * 16)
        tk_pc_wo = (1, 4 * 16)
        tk_pc_wb = (2, 4 * 16)

        ps_g = p1a.enter_context(nc.psum_tensor("ps_g", [128, 1], F32))
        dCN.wait(PE, tk_cn)
        PE.wait(PL, t_p0pl)
        PE.op(nc.tensor.matmul(ps_g[:], onesf[:], gtmp[:], start=True, stop=True))
        t_gbc = PE.mark()
        SC.wait(PE, t_gbc)
        SC.wait(PL, t_p0pl)
        SC.op(nc.scalar.activation(gcol[:], ps_g[:], AF.Exp, scale=-1.0))
        t_ge = SC.mark()
        VE.wait(SC, t_ge)
        VE.op(nc.vector.tensor_scalar_add(gcol[:], gcol[:], 1.0))
        t_g1 = VE.mark()
        VE.h.wait_ge(VE.sem, t_g1)
        VE.op(nc.vector.reciprocal(gcol[:], gcol[:]))
        t_g2 = VE.mark()
        VE.h.wait_ge(VE.sem, t_g2)
        VE.op(nc.vector.tensor_scalar(g1col[:], gcol[:], -1.0, 1.0, ALU.mult, ALU.add))
        t_g3 = VE.mark()

        xt = sb1a("xt", [128, 2, D])
        ln_bf = sb1a("ln_bf", [128, 2, D], BF16)
        lnT4 = sb1a("lnT4", [128, NF, 512], BF16)
        bn6 = sb1a("bn6", [128, 2, 8, 6])
        bnst = sb1a("bnst", [128, 2, 2])
        rcol = sb1a("rcol", [128, 2])
        mbcol = sb1a("mbcol", [128, 2])
        vfm_bf = sb1a("vfm_bf", [128, 512], BF16)
        ps_k = p1a.enter_context(nc.psum_tensor("ps_k", [128, 512], F32))
        ps_v = p1a.enter_context(nc.psum_tensor("ps_v", [128, 512], F32))
        psT = [p1a.enter_context(nc.psum_tensor(f"ps_t{k}", [128, 8, 128], BF16))
               for k in range(4)]

        state = {
            "xt_user": [[], []],
            "lnbf_user": [0, 0],
            "pst_evac": [0, 0, 0, 0],
            "pst_rr": [0],
            "stats_user": [0, 0],
        }

        def ln_tile_pipeline(src_ap, par, dma, dtok, lnT_dst, dst_free_tok=0,
                             ve_apply=False):
            dma.wait(VE, dtok)
            if state["stats_user"][par]:
                e_, t_ = state["stats_user"][par]
                VE.wait(e_, t_) if e_ is not VE else VE.h.wait_ge(VE.sem, t_)
            for g8 in range(8):
                VE.op(nc.vector.bn_stats(bn6[:, par, g8, :],
                                         src_ap[:, 512 * g8:512 * (g8 + 1)]))
            t_bn = VE.mark()
            VE.h.wait_ge(VE.sem, t_bn)
            VE.op(nc.vector.bn_aggr(bnst[:, par, :], bn6[:, par, :, :]))
            t_ag = VE.mark()
            SC.wait(VE, t_ag)
            SC.op(nc.scalar.activation(rcol[:, par:par + 1], bnst[:, par, 1:2],
                                       AF.Ln, bias=eps1[:]))
            t_l = SC.mark()
            SC.h.wait_ge(SC.sem, t_l)
            SC.op(nc.scalar.activation(rcol[:, par:par + 1], rcol[:, par:par + 1],
                                       AF.Exp, scale=-0.5))
            t_r = SC.mark()
            if ve_apply:
                VE.wait(SC, t_r)
                if state["lnbf_user"][par]:
                    VE.wait(PE, state["lnbf_user"][par])
                VE.op(nc.vector.tensor_scalar(ln_bf[:, par, :], src_ap,
                                              bnst[:, par, 0:1], rcol[:, par:par + 1],
                                              ALU.subtract, ALU.mult))
                t_ap = VE.mark()
                state["stats_user"][par] = (VE, t_ap)
                PE.wait(VE, t_ap)
            else:
                VE.wait(SC, t_r)
                VE.op(nc.vector.tensor_scalar(mbcol[:, par:par + 1], bnst[:, par, 0:1],
                                              rcol[:, par:par + 1], -1.0,
                                              ALU.mult, ALU.mult))
                t_mb = VE.mark()
                SC.wait(VE, t_mb)
                if state["lnbf_user"][par]:
                    SC.wait(PE, state["lnbf_user"][par])
                SC.op(nc.scalar.activation(ln_bf[:, par, :], src_ap, AF.Identity,
                                           bias=mbcol[:, par:par + 1],
                                           scale=rcol[:, par:par + 1]))
                t_ap = SC.mark()
                state["stats_user"][par] = (SC, t_ap)
                PE.wait(SC, t_ap)
            for fg in range(4):
                rr = state["pst_rr"][0] % 4
                state["pst_rr"][0] += 1
                pst = psT[rr]
                ev_eng = SC if (fg % 2 == 0) else VE
                if state["pst_evac"][rr]:
                    e_, t_ = state["pst_evac"][rr]
                    PE.wait(e_, t_)
                for ff in range(8):
                    f = 8 * fg + ff
                    PE.op(nc.tensor.transpose(pst[:, ff, :],
                                              ln_bf[:, par, 128 * f:128 * (f + 1)],
                                              ident[:]))
                t_tr = PE.mark()
                ev_eng.wait(PE, t_tr)
                if fg < 2 and dst_free_tok:
                    ev_eng.wait(PE, dst_free_tok)
                if ev_eng is SC:
                    ev_eng.op(nc.scalar.copy(lnT_dst(fg), pst[:, :, :]))
                else:
                    ev_eng.op(nc.vector.tensor_copy(lnT_dst(fg), pst[:, :, :]))
                state["pst_evac"][rr] = (ev_eng, ev_eng.mark())
            state["lnbf_user"][par] = PE.n
            return (SC.n, VE.n), t_ap, t_ag

        tok_kv_prev = 0
        tok_vfm_user = 0
        if BUILD_PHASES >= 1:
            for g4 in range(P1A_GROUPS):
                last_ev = 0
                for tt in range(4):
                    t = 4 * g4 + tt
                    par = t % 2
                    for (ue, ut) in state["xt_user"][par]:
                        SP.wait(ue, ut)
                    tk = dXK.start(t, xt[:, par, :], xk[128 * t:128 * (t + 1), :])

                    def lnT_dst(fg, tt=tt):
                        return lnT4[:, 8 * fg:8 * fg + 8, 128 * tt:128 * (tt + 1)]

                    last_ev, t_ap, t_ag = ln_tile_pipeline(
                        xt[:, par, :], par, dXK, tk, lnT_dst,
                        dst_free_tok=tok_kv_prev if tt == 0 else 0,
                        ve_apply=(t % 2 == 1))
                    ap_eng = VE if (t % 2 == 1) else SC
                    state["xt_user"][par] = [(ap_eng, t_ap), (VE, t_ag)]
                PE.wait(SC, last_ev[0])
                PE.wait(VE, last_ev[1])
                gW.wait(PE, tk_w0)
                for f in range(NF):
                    PE.op(nc.tensor.matmul(ps_k[:], wkv_bf[:, f, 0:128], lnT4[:, f, :],
                                           start=(f == 0), stop=(f == NF - 1)))
                    PE.op(nc.tensor.matmul(ps_v[:], wkv_bf[:, f, 128:256], lnT4[:, f, :],
                                           start=(f == 0), stop=(f == NF - 1)))
                t_kv = PE.mark()
                tok_kv_prev = t_kv
                SC.wait(PE, t_kv)
                SC.op(nc.scalar.copy(kfm[:, 512 * g4:512 * (g4 + 1)], ps_k[:]))
                t_ke = SC.mark()
                SC.h.wait_ge(SC.sem, t_ke)
                if tok_vfm_user:
                    SC.wait(PE, tok_vfm_user)
                SC.op(nc.scalar.copy(vfm_bf[:], ps_v[:]))
                t_vev = SC.mark()
                PE.wait(SC, t_vev)
                for tt in range(4):
                    rr = state["pst_rr"][0] % 4
                    state["pst_rr"][0] += 1
                    pst = psT[rr]
                    if state["pst_evac"][rr]:
                        e_, t_ = state["pst_evac"][rr]
                        PE.wait(e_, t_)
                    PE.op(nc.tensor.transpose(pst[:, 0, :],
                                              vfm_bf[:, 128 * tt:128 * (tt + 1)], ident[:]))
                    t_tv = PE.mark()
                    SC.wait(PE, t_tv)
                    SC.op(nc.scalar.copy(vtiles[:, 4 * g4 + tt, 0:128], pst[:, 0, :]))
                    state["pst_evac"][rr] = (SC, SC.mark())
                tok_vfm_user = PE.n

            for t in range(NT if P1A_OWN else 0):
                par = t % 2
                for (ue, ut) in state["xt_user"][par]:
                    SP.wait(ue, ut)
                tk = dXQ.start(t, xt[:, par, :], xq[128 * t:128 * (t + 1), :])

                def lnT_dst(fg, t=t):
                    return lnT_own[:, 8 * fg:8 * fg + 8, 128 * t:128 * (t + 1)]

                _, t_ap, t_ag = ln_tile_pipeline(xt[:, par, :], par, dXQ, tk, lnT_dst,
                                                 ve_apply=(t % 2 == 1))
                ap_eng = VE if (t % 2 == 1) else SC
                state["xt_user"][par] = [(ap_eng, t_ap), (VE, t_ag)]

        tok_lnT_own_ready = SC.mark()
        tok_lnT_own_ready_ve = VE.mark()

        if DEBUG and BUILD_PHASES >= 1:
            barrier()
            SP.h.wait_ge(SC.sem, tok_lnT_own_ready)
            gPC.start(3, taps["t_kfm"][:], kfm[:])
            gPC.start(3, taps["t_v"][:], vtiles[:])
            gPC.start(3, taps["t_lnT"][:], lnT_own[:])
            tap_quiesce()

        if BUILD_PHASES >= 2:
            barrier()
            p1a.close()
            attn_out = pa.enter_context(
                nc.sbuf_tensor("attn_out", [128, NT, D], BF16, side="right"))
            dmask_s = sb1("dmask_s", [128, WD], F16)
            gW.qh.wait_ge(gW.sems[0], gW.cnt[0])
            tk_dm = gW.start(0, dmask_s[:], dmask[:])
            q_bf = sb1("q_bf", [128, 2, T], BF16)
            probs = sb1("probs", [128, 3, 512])
            probsT = sb1("probsT", [128, 3, 512], BF16)
            wq_bf = sb1("wq_bf", [128, 2, NF, 128], BF16)
            rec = sb1("rec", [128, 8])
            ps_q = p1.enter_context(nc.psum_tensor("ps_q", [128, 1024], F32))
            ps_s = p1.enter_context(nc.psum_tensor("ps_s", [128, 2, 512], F32))
            pvt = [p1.enter_context(nc.psum_tensor(f"pv{k}", [128, 129], F32))
                   for k in range(4)]
            dmv2 = dmask_s[:].rearrange("p (a b) -> p a b", b=128)

            tok_wq = [None, None]
            tok_qevac_last = 0
            tok_qbf_user = [0, 0]
            tok_ps_s_user = [0, 0]
            tok_probs_user = [0, 0, 0]
            tok_probsT_user = [0, 0, 0]
            tok_pv_user = [0] * 4
            tok_rec_user = [0]
            tok_qgen_done = 0

            sWQ.qh.wait_ge(gPC.sems[tk_pc_q[0]], tk_pc_q[1])
            tok_wq[0] = sWQ.start(0, wq_bf[:, 0, :, :],
                                  wq_bf_d[:, 0:128].rearrange("(c p) n -> p c n", p=128))
            gW.wait(VE, tk_dm)

            sjob = 0

            def emit_qgen(hh, par_):
                acts = []
                first = {"v": True}

                def mk(f):
                    def go():
                        if first["v"]:
                            sWQ.wait(PE, tok_wq[par_])
                            PE.wait(SC, tok_lnT_own_ready)
                            PE.wait(VE, tok_lnT_own_ready_ve)
                            if tok_qevac_last[0]:
                                PE.wait(SC, tok_qevac_last[0])
                            first["v"] = False
                        PE.op(nc.tensor.matmul(ps_q[:, 0:512], wq_bf[:, par_, f, :],
                                               lnT_own[:, f, 0:512],
                                               start=(f == 0), stop=(f == NF - 1)))
                        PE.op(nc.tensor.matmul(ps_q[:, 512:1024], wq_bf[:, par_, f, :],
                                               lnT_own[:, f, 512:1024],
                                               start=(f == 0), stop=(f == NF - 1)))
                    return go
                for f in range(NF):
                    acts.append(mk(f))
                return acts

            tok_qevac_last = [0]

            for th in emit_qgen(0, 0):
                th()
            tok_qgen_done = PE.mark()
            SC.wait(PE, tok_qgen_done)
            SC.op(nc.scalar.activation(q_bf[:, 0, :], ps_q[:], AF.Identity,
                                       scale=1.0 / (SLOPES[0] * math.sqrt(HD))))
            tok_qevac_last[0] = SC.mark()
            tok_q_cur = tok_qevac_last[0]

            pending_wq = [1]

            def issue_wq():
                hh = pending_wq[0]
                if hh < NH:
                    sWQ.qh.wait_ge(PE.sem, tok_qgen_done)
                    tok_wq[hh % 2] = sWQ.start(hh % 2, wq_bf[:, hh % 2, :, :],
                                               wq_bf_d[:, 128 * hh:128 * (hh + 1)]
                                               .rearrange("(c p) n -> p c n", p=128))
                    pending_wq[0] = hh + 1

            issue_wq()
            for h in range(NH):
                par = h % 2
                nxt = (h + 1) % 2
                filler = emit_qgen(h + 1, nxt) if h + 1 < NH else []
                fill_i = 0

                nw = NW[h]
                slope = SLOPES[h]
                jobs = []
                for grp in range(2):
                    glo, ghi = 4 * grp, 4 * grp + 3
                    ranges = []
                    for kc in range(NKC):
                        i_lo, i_hi = _i_range(kc, nw)
                        i0, i1 = max(i_lo, glo), min(i_hi, ghi)
                        if i0 <= i1:
                            ranges.append((kc, i0, i1))
                    k = 0
                    while k < len(ranges):
                        kcs = [ranges[k][0]]
                        ri0, ri1 = ranges[k][1], ranges[k][2]
                        k2 = k + 1
                        while k2 < len(ranges) and ranges[k2][0] == kcs[-1] + 1:
                            n_i0 = min(ri0, ranges[k2][1])
                            n_i1 = max(ri1, ranges[k2][2])
                            if (len(kcs) + 1) * 128 * (n_i1 - n_i0 + 1) > 512:
                                break
                            kcs.append(ranges[k2][0])
                            ri0, ri1 = n_i0, n_i1
                            k2 += 1
                        jobs.append((list(reversed(kcs)), ri0, ri1, glo))
                        k = k2

                def emit_scores_dve_act(job, sj):
                    kcs, i0, i1, glo = job
                    nch = i1 - i0 + 1
                    rcols = 128 * nch
                    nkc = len(kcs)
                    cols = rcols * nkc
                    slot = sj % 2
                    pp = sj % 3
                    if tok_ps_s_user[slot]:
                        PE.wait(VE, tok_ps_s_user[slot])
                    for kx, kc in enumerate(kcs):
                        PE.op(nc.tensor.matmul(
                            ps_s[:, slot, rcols * kx:rcols * (kx + 1)],
                            kfm[:, 128 * kc:128 * (kc + 1)],
                            q_bf[:, par, 128 * i0:128 * i0 + rcols],
                            start=True, stop=True))
                    t_sc = PE.mark()
                    VE.wait(PE, t_sc)
                    if tok_probs_user[pp]:
                        VE.wait(SC, tok_probs_user[pp])
                    base_hi = C0 + 128 * (4 * i0 - kcs[0])
                    s0 = base_hi // 128
                    dview = dmv2[:, s0:s0 + 4 * nch] \
                        .rearrange("p (c k) b -> p c k b", k=4) \
                        .rearrange("p c k b -> p k c b")[:, 0:nkc, :, :]
                    pview = probs[:, pp, 0:cols].rearrange(
                        "p (a b c) -> p a b c", a=nkc, c=128)
                    sview = ps_s[:, slot, 0:cols].rearrange(
                        "p (a b c) -> p a b c", a=nkc, c=128)
                    VE.op(nc.vector.tensor_tensor(pview, sview, dview, ALU.add))
                    t_da = VE.mark()
                    tok_ps_s_user[slot] = t_da
                    SC.wait(VE, t_da)
                    if tok_probsT_user[pp]:
                        SC.wait(PE, tok_probsT_user[pp])
                    SC.op(nc.scalar.activation(probsT[:, pp, 0:cols],
                                               probs[:, pp, 0:cols], AF.Exp,
                                               scale=slope))
                    t_ex = SC.mark()
                    tok_probs_user[pp] = t_ex
                    return t_ex

                def emit_pv(job, sj, t_ex):
                    kcs, i0, i1, glo = job
                    nch = i1 - i0 + 1
                    rcols = 128 * nch
                    pp = sj % 3
                    PE.wait(SC, t_ex)
                    stopped = []
                    for kx in range(len(kcs) - 1, -1, -1):
                        kc = kcs[kx]
                        gi0, gi1 = _i_range(kc, nw)
                        for ii in range(nch):
                            i = i0 + ii
                            if i < max(gi0, glo) or i > min(gi1, glo + 3):
                                continue
                            st = (kc == _start_kc(i, nw))
                            spf = (kc == 4 * i + 3)
                            if spf:
                                stopped.append(i)
                            if st and tok_pv_user[i - glo]:
                                PE.wait(SC, tok_pv_user[i - glo])
                            PE.op(nc.tensor.matmul(
                                pvt[i - glo][:],
                                probsT[:, pp, rcols * kx + 128 * ii:
                                       rcols * kx + 128 * (ii + 1)],
                                vtiles[:, kc, :], start=st, stop=spf))
                    t_pvt = PE.mark()
                    tok_probsT_user[pp] = t_pvt
                    for i in stopped:
                        if True:
                            il = i - glo
                            VE.wait(PE, t_pvt)
                            if tok_rec_user[0]:
                                VE.wait(SC, tok_rec_user[0])
                            VE.op(nc.vector.reciprocal(rec[:, i:i + 1],
                                                       pvt[il][:][:, 128:129]))
                            t_rc = VE.mark()
                            SC.wait(VE, t_rc)
                            SC.op(nc.scalar.activation(
                                attn_out[:, i, 128 * h:128 * (h + 1)],
                                pvt[il][:][:, 0:128], AF.Identity,
                                scale=rec[:, i:i + 1]))
                            tok_pv_user[il] = SC.mark()
                            tok_rec_user[0] = tok_pv_user[il]

                prev = None
                first_mm = True
                for job in jobs:
                    if first_mm:
                        PE.wait(SC, tok_q_cur)
                        first_mm = False
                    t_ex = emit_scores_dve_act(job, sjob)
                    if prev is not None:
                        emit_pv(prev[0], prev[1], prev[2])
                    for _ in range(3):
                        if fill_i < len(filler):
                            filler[fill_i]()
                            fill_i += 1
                    if fill_i >= len(filler) and pending_wq[0] == h + 2:
                        tok_qgen_done = PE.mark()
                        issue_wq()
                    prev = (job, sjob, t_ex)
                    sjob += 1
                if prev is not None:
                    emit_pv(prev[0], prev[1], prev[2])
                while fill_i < len(filler):
                    filler[fill_i]()
                    fill_i += 1
                tok_qbf_user[par] = PE.n
                if h + 1 < NH:
                    tok_qgen_done = PE.mark()
                    if pending_wq[0] == h + 2:
                        issue_wq()
                    SC.wait(PE, tok_qgen_done)
                    if tok_qbf_user[nxt]:
                        SC.wait(PE, tok_qbf_user[nxt])
                    SC.op(nc.scalar.activation(
                        q_bf[:, nxt, :], ps_q[:], AF.Identity,
                        scale=1.0 / (SLOPES[h + 1] * math.sqrt(HD))))
                    tok_qevac_last[0] = SC.mark()
                    tok_q_cur = tok_qevac_last[0]

        if DEBUG and BUILD_PHASES >= 2:
            barrier()
            SP.h.wait_ge(SC.sem, SC.n)
            gPC.start(3, taps["t_attn"][:], attn_out[:])
            tap_quiesce()

        x_res = None
        if BUILD_PHASES >= 3:
            barrier()
            p1.close()
            p1c = ExitStack()
            sbc = lambda nm, shp, dt=F32: p1c.enter_context(nc.sbuf_tensor(nm, shp, dt))
            attn_outT = sbc("attn_outT", [128, NF, T], BF16)
            pt = ExitStack()
            psTc = [pt.enter_context(nc.psum_tensor(f"ps_tc{k}", [128, 8, 128], BF16))
                    for k in range(4)]
            tok_ev = [(None, 0)] * 4
            rrc = 0
            for t in range(NT):
                for fg in range(4):
                    rr = rrc % 4
                    rrc += 1
                    pst = psTc[rr]
                    ev_eng = SC if (fg % 2 == 0) else VE
                    e_, t_ = tok_ev[rr]
                    if e_ is not None:
                        PE.wait(e_, t_)
                    for ff in range(8):
                        f = 8 * fg + ff
                        PE.op(nc.tensor.transpose(pst[:, ff, :],
                                                  attn_out[:, t, 128 * f:128 * (f + 1)],
                                                  ident[:]))
                    t_tr = PE.mark()
                    ev_eng.wait(PE, t_tr)
                    if ev_eng is SC:
                        ev_eng.op(nc.scalar.copy(
                            attn_outT[:, 8 * fg:8 * fg + 8, 128 * t:128 * (t + 1)],
                            pst[:, :, :]))
                    else:
                        ev_eng.op(nc.vector.tensor_copy(
                            attn_outT[:, 8 * fg:8 * fg + 8, 128 * t:128 * (t + 1)],
                            pst[:, :, :]))
                    tok_ev[rr] = (ev_eng, ev_eng.mark())
            barrier()
            pa.close()
            pt.close()
            x_res = top.enter_context(
                nc.sbuf_tensor("x_res", [128, NT, D], F32, side="right"))

            ps_wo = p1c.enter_context(nc.psum_tensor("ps_wo", [128, NT, 512], F32))
            wo_bf = sbc("wo_bf", [128, 2, 2, 512], BF16)
            xs_buf = sbc("xs_buf", [128, 2, 512])
            tok_wo = [None, None]
            tok_wo_user = [0, 0]
            tok_xs_user = [0, 0]
            tok_pswo_user = [0] * NT
            for os_ in range(8):
                for fg in range(16):
                    sl = (os_ * 16 + fg) % 2
                    if tok_wo_user[sl]:
                        gWO.qh.wait_ge(PE.sem, tok_wo_user[sl])
                    if os_ == 0 and fg == 0:
                        gWO.qh.wait_ge(gPC.sems[tk_pc_wo[0]], tk_pc_wo[1])
                    tok_wo[sl] = gWO.start(sl, wo_bf[:, sl, :, :],
                                           wo_bf_d[256 * fg:256 * (fg + 1),
                                                   512 * os_:512 * (os_ + 1)]
                                           .rearrange("(a p) n -> p a n", p=128))
                    gWO.wait(PE, tok_wo[sl])
                    for ff in range(2):
                        f = 2 * fg + ff
                        for t in range(NT):
                            st = (f == 0)
                            spf = (f == NF - 1)
                            if st and tok_pswo_user[t]:
                                PE.wait(VE, tok_pswo_user[t])
                            PE.op(nc.tensor.matmul(ps_wo[:, t, :],
                                                   attn_outT[:, f, 128 * t:128 * (t + 1)],
                                                   wo_bf[:, sl, ff, :],
                                                   start=st, stop=spf))
                    tok_wo_user[sl] = PE.mark()
                t_wod = PE.n
                for t in range(NT):
                    spar = t % 2
                    if tok_xs_user[spar]:
                        SP.wait(VE, tok_xs_user[spar])
                    tkx = dXQ.start(spar, xs_buf[:, spar, :],
                                    xq[128 * t:128 * (t + 1), 512 * os_:512 * (os_ + 1)])
                    dXQ.wait(VE, tkx)
                    VE.wait(PE, t_wod)
                    VE.op(nc.vector.tensor_tensor(
                        x_res[:, t, 512 * os_:512 * (os_ + 1)],
                        ps_wo[:, t, :], xs_buf[:, spar, :], ALU.add))
                    t_x1 = VE.mark()
                    tok_pswo_user[t] = t_x1
                    tok_xs_user[spar] = t_x1
            barrier()
            p1c.close()
            if DEBUG:
                SP.h.wait_ge(VE.sem, VE.n)
                dST.start(0, taps["t_x1"][:], x_res[:])
                tap_quiesce()

        def emit_ffn(store_out):
            tag = "b" if store_out else "a"
            pf = ExitStack()
            sbf = lambda nm, shp, dt=F32: pf.enter_context(
                nc.sbuf_tensor(nm + tag, shp, dt))
            xa = sbf("xa", [128, D])
            xa_bf = sbf("xa_bf", [128, D], BF16)
            xaT = sbf("xaT", [128, NF, 128], BF16)
            Lbuf = sbf("Lbuf", [128, NT, 24])
            stat_m = sbf("stat_m", [128, NT])
            stat_v = sbf("stat_v", [128, NT])
            stat_r = sbf("stat_r", [128, NT])
            bn6f = sbf("bn6f", [128, 8, 6])
            bnsf = sbf("bnsf", [128, 2])
            mbf = sbf("mbf", [128, 1])
            rmsc = sbf("rmsc", [128, 1])
            Z = sbf("Z", [128, NT, 4, 4])
            ub = sbf("ub", [128, NT, 4])
            vb = sbf("vb", [128, NT, 4])
            Tb = sbf("Tb", [128, NT, 4, 4])
            Kp = sbf("Kp", [128, NT, 4, 4])
            Mb = sbf("Mb", [128, NT, 4])
            Sb = sbf("Sb", [128, NT, 4])
            Lb = sbf("Lb", [128, NT, 4])
            smax = sbf("smax", [128, NT])
            hpre = sbf("hpre", [128, NT, 4])
            hpost = sbf("hpost", [128, NT, 4])
            hres = sbf("hres", [128, NT, 16])
            xp = sbf("xp", [128, 1024])
            xp_bf = sbf("xp_bf", [128, 1024], BF16)
            xpT = sbf("xpT", [128, 8, 128], BF16)
            y_f32 = sbf("y_f32", [128, 1024])
            wbr_s = sbf("wbr_s", [128, 8, 1024], BF16)
            gW.qh.wait_ge(gW.sems[1], gW.cnt[1])
            tk_wbr = gW.start(1, wbr_s[:], wbr[:].rearrange("(c p) n -> p c n", p=128))
            ps_L = pf.enter_context(nc.psum_tensor("ps_L" + tag, [128, 24], F32))
            ps_tf0 = pf.enter_context(nc.psum_tensor("ps_tf0" + tag, [128, 8, 128], BF16))
            ps_tf1 = pf.enter_context(nc.psum_tensor("ps_tf1" + tag, [128, 8, 128], BF16))
            ps_y = pf.enter_context(nc.psum_tensor("ps_y" + tag, [128, 1024], F32))

            tA_ev = [0, 0]
            tok_bn_user_ve = 0
            tok_bn_user_sc = 0
            tok_xabf_user = 0
            tok_psL_user = 0
            tok_rms_user = 0
            for t in range(NT):
                if tok_bn_user_ve:
                    VE.h.wait_ge(VE.sem, tok_bn_user_ve)
                if tok_bn_user_sc:
                    VE.wait(SC, tok_bn_user_sc)
                for g8 in range(8):
                    VE.op(nc.vector.bn_stats(bn6f[:, g8, :],
                                             x_res[:, t, 512 * g8:512 * (g8 + 1)]))
                t_bn = VE.mark()
                VE.h.wait_ge(VE.sem, t_bn)
                VE.op(nc.vector.bn_aggr(bnsf[:], bn6f[:]))
                t_ag = VE.mark()
                VE.h.wait_ge(VE.sem, t_ag)
                VE.op(nc.vector.tensor_copy(stat_m[:, t:t + 1], bnsf[:, 0:1]))
                VE.op(nc.vector.tensor_copy(stat_v[:, t:t + 1], bnsf[:, 1:2]))
                t_cp = VE.mark()
                tok_bn_user_ve = t_cp
                SC.wait(VE, t_cp)
                SC.op(nc.scalar.activation(stat_r[:, t:t + 1], stat_v[:, t:t + 1],
                                           AF.Ln, bias=eps1[:]))
                t_l1 = SC.mark()
                SC.h.wait_ge(SC.sem, t_l1)
                SC.op(nc.scalar.activation(stat_r[:, t:t + 1], stat_r[:, t:t + 1],
                                           AF.Exp, scale=-0.5))
                t_r = SC.mark()
                tok_bn_user_sc = t_r
                VE.wait(SC, t_r)
                if tok_rms_user:
                    VE.wait(SC, tok_rms_user)
                VE.op(nc.vector.tensor_scalar(rmsc[:], stat_v[:, t:t + 1],
                                              stat_r[:, t:t + 1], stat_r[:, t:t + 1],
                                              ALU.mult, ALU.mult))
                t_r2 = VE.mark()
                SC.wait(VE, t_r2)
                SC.op(nc.scalar.activation(rmsc[:], rmsc[:], AF.Ln, bias=epsr[:]))
                t_l2 = SC.mark()
                SC.h.wait_ge(SC.sem, t_l2)
                SC.op(nc.scalar.activation(rmsc[:], rmsc[:], AF.Exp, scale=-0.5))
                t_rms = SC.mark()
                VE.wait(SC, t_r)
                VE.op(nc.vector.tensor_scalar(mbf[:], stat_m[:, t:t + 1],
                                              stat_r[:, t:t + 1], -1.0,
                                              ALU.mult, ALU.mult))
                t_mb = VE.mark()
                SC.wait(VE, t_mb)
                if tok_xabf_user:
                    SC.wait(PE, tok_xabf_user)
                SC.op(nc.scalar.activation(xa_bf[:], x_res[:, t, :], AF.Identity,
                                           bias=mbf[:], scale=stat_r[:, t:t + 1]))
                t_xa = SC.mark()
                PE.wait(SC, t_xa)
                for fg in range(4):
                    pp = fg % 2
                    pst = ps_tf0 if pp == 0 else ps_tf1
                    if tA_ev[pp]:
                        PE.wait(SC, tA_ev[pp])
                    for ff in range(8):
                        f = 8 * fg + ff
                        PE.op(nc.tensor.transpose(pst[:, ff, :],
                                                  xa_bf[:, 128 * f:128 * (f + 1)],
                                                  ident[:]))
                    t_tr = PE.mark()
                    SC.wait(PE, t_tr)
                    SC.op(nc.scalar.copy(xaT[:, 8 * fg:8 * fg + 8, :], pst[:, :, :]))
                    tA_ev[pp] = SC.mark()
                tok_xabf_user = PE.n
                PE.wait(SC, tA_ev[0])
                PE.wait(SC, tA_ev[1])
                gW.wait(PE, tk_w0)
                if tok_psL_user:
                    PE.wait(SC, tok_psL_user)
                for f in range(NF):
                    PE.op(nc.tensor.matmul(ps_L[:], xaT[:, f, :], wrpp_bf[:, f, :],
                                           start=(f == 0), stop=(f == NF - 1)))
                t_lg = PE.mark()
                SC.wait(PE, t_lg)
                SC.h.wait_ge(SC.sem, t_rms)
                SC.op(nc.scalar.activation(Lbuf[:, t, :], ps_L[:], AF.Identity,
                                           scale=rmsc[:]))
                t_le = SC.mark()
                tok_psL_user = t_le
                tok_rms_user = t_le
                dCN.wait(VE, tk_cn)
                VE.wait(SC, t_le)
                VE.op(nc.vector.tensor_tensor(Lbuf[:, t, :], Lbuf[:, t, :], bspp[:],
                                              ALU.add))
                VE.mark()

            t_A = VE.n
            VE.h.wait_ge(VE.sem, t_A)
            last_exp_sc = 0
            for (lo, dst) in ((16, hpre), (20, hpost)):
                VE.h.wait_ge(VE.sem, VE.n)
                Lv = Lbuf[:, :, lo:lo + 4]
                VE.op(nc.vector.reduce_max(
                    smax[:].rearrange("p (a o) -> p a o", o=1), Lv, AX.X))
                t_m = VE.mark()
                VE.h.wait_ge(VE.sem, t_m)
                smb = smax[:].rearrange("p (a o) -> p a o", o=1).broadcast_to([128, NT, 4])
                VE.op(nc.vector.tensor_sub(dst[:], Lv, smb))
                t_s = VE.mark()
                SC.wait(VE, t_s)
                SC.op(nc.scalar.activation(dst[:], dst[:], AF.Exp))
                t_e = SC.mark()
                last_exp_sc = t_e
                VE.wait(SC, t_e)
                VE.op(nc.vector.reduce_sum(
                    smax[:].rearrange("p (a o) -> p a o", o=1), dst[:], AX.X))
                t_ss = VE.mark()
                VE.h.wait_ge(VE.sem, t_ss)
                VE.op(nc.vector.reciprocal(smax[:], smax[:]))
                t_rcp = VE.mark()
                VE.h.wait_ge(VE.sem, t_rcp)
                VE.op(nc.vector.tensor_mul(dst[:], dst[:], smb))
                VE.mark()
            VE.op(nc.vector.tensor_scalar_mul(
                Z[:].rearrange("p a i j -> p a (i j)"), Lbuf[:, :, 0:16], TAU_INV))
            t_z = VE.mark()
            VE.h.wait_ge(VE.sem, t_z)
            PL.op(nc.gpsimd.memset(ub[:], 0.0))
            PL.op(nc.gpsimd.memset(vb[:], 0.0))
            t_z0 = PL.mark()
            VE.wait(PL, t_z0)

            st_sink = {"t4": last_exp_sc}

            def half_iter(z_view, pot_in, pot_out):
                if st_sink["t4"]:
                    VE.wait(SC, st_sink["t4"])
                pib = pot_in[:].rearrange("p a (o i) -> p a o i", o=1) \
                    .broadcast_to([128, NT, 4, 4])
                VE.op(nc.vector.tensor_sub(Tb[:], z_view, pib))
                t1 = VE.mark()
                VE.h.wait_ge(VE.sem, t1)
                VE.op(nc.vector.reduce_max(Mb[:], Tb[:], AX.X))
                t2 = VE.mark()
                VE.h.wait_ge(VE.sem, t2)
                mb2 = Mb[:].rearrange("p a (i o) -> p a i o", o=1) \
                    .broadcast_to([128, NT, 4, 4])
                VE.op(nc.vector.tensor_sub(Tb[:], Tb[:], mb2))
                t3 = VE.mark()
                SC.wait(VE, t3)
                SC.op(nc.scalar.activation(Tb[:], Tb[:], AF.Exp))
                t4 = SC.mark()
                st_sink["t4"] = t4
                VE.wait(SC, t4)
                VE.op(nc.vector.reduce_sum(Sb[:], Tb[:], AX.X))
                t5 = VE.mark()
                SC.wait(VE, t5)
                SC.op(nc.scalar.activation(Lb[:], Sb[:], AF.Ln))
                t6 = SC.mark()
                VE.wait(SC, t6)
                VE.op(nc.vector.tensor_add(pot_out[:], Lb[:], Mb[:]))
                t7 = VE.mark()
                VE.h.wait_ge(VE.sem, t7)

            zv = Z[:]
            zvT = Z[:].rearrange("p a i j -> p a j i")
            half_iter(zv, vb, ub)
            half_iter(zvT, ub, vb)
            VE.wait(SC, st_sink["t4"])
            uib = ub[:].rearrange("p a (i o) -> p a i o", o=1).broadcast_to([128, NT, 4, 4])
            vjb = vb[:].rearrange("p a (o i) -> p a o i", o=1).broadcast_to([128, NT, 4, 4])
            VE.op(nc.vector.tensor_sub(Tb[:], Z[:], uib))
            t_k1 = VE.mark(); VE.h.wait_ge(VE.sem, t_k1)
            VE.op(nc.vector.tensor_sub(Tb[:], Tb[:], vjb))
            t_k2 = VE.mark()
            SC.wait(VE, t_k2)
            SC.op(nc.scalar.activation(Kp[:], Tb[:], AF.Exp))
            t_kp = SC.mark()
            VE.wait(SC, t_kp)
            VE.op(nc.vector.memset(ub[:], 1.0))
            VE.op(nc.vector.memset(vb[:], 1.0))
            t_ab0 = VE.mark(); VE.h.wait_ge(VE.sem, t_ab0)
            KpT = Kp[:].rearrange("p a i j -> p a j i")
            for _ in range(SINK_ITERS - 1):
                bb = vb[:].rearrange("p a (o i) -> p a o i", o=1) \
                    .broadcast_to([128, NT, 4, 4])
                VE.op(nc.vector.tensor_mul(Tb[:], Kp[:], bb))
                t1 = VE.mark(); VE.h.wait_ge(VE.sem, t1)
                VE.op(nc.vector.reduce_sum(Sb[:], Tb[:], AX.X))
                t2 = VE.mark(); VE.h.wait_ge(VE.sem, t2)
                VE.op(nc.vector.reciprocal(ub[:], Sb[:]))
                t3 = VE.mark(); VE.h.wait_ge(VE.sem, t3)
                ab = ub[:].rearrange("p a (o i) -> p a o i", o=1) \
                    .broadcast_to([128, NT, 4, 4])
                VE.op(nc.vector.tensor_mul(Tb[:], KpT, ab))
                t4 = VE.mark(); VE.h.wait_ge(VE.sem, t4)
                VE.op(nc.vector.reduce_sum(Sb[:], Tb[:], AX.X))
                t5 = VE.mark(); VE.h.wait_ge(VE.sem, t5)
                VE.op(nc.vector.reciprocal(vb[:], Sb[:]))
                t6 = VE.mark(); VE.h.wait_ge(VE.sem, t6)
            bb = vb[:].rearrange("p a (o i) -> p a o i", o=1).broadcast_to([128, NT, 4, 4])
            VE.op(nc.vector.tensor_mul(Tb[:], Kp[:], bb))
            t_h1 = VE.mark(); VE.h.wait_ge(VE.sem, t_h1)
            aib = ub[:].rearrange("p a (i o) -> p a i o", o=1).broadcast_to([128, NT, 4, 4])
            VE.op(nc.vector.tensor_mul(
                hres[:].rearrange("p a (i j) -> p a i j", j=4), Tb[:], aib))
            t_h4 = VE.mark(); VE.h.wait_ge(VE.sem, t_h4)
            VE.wait(VE, t_g3)
            VE.op(nc.vector.tensor_scalar_mul(hres[:], hres[:], gcol[:]))
            t_h5 = VE.mark(); VE.h.wait_ge(VE.sem, t_h5)
            for dd in range(4):
                VE.op(nc.vector.tensor_scalar_add(hres[:, :, 5 * dd:5 * dd + 1],
                                                  hres[:, :, 5 * dd:5 * dd + 1],
                                                  g1col[:]))
            t_hres = VE.mark()

            gW.wait(PE, tk_wbr)
            tok_xa_user = 0
            tok_y_user = 0
            tok_psy_sc = 0
            tok_xpbf_user = 0
            tok_xpT_user = 0
            st_toks = []
            for t in range(NT):
                VE.h.wait_ge(VE.sem, t_hres)
                if tok_xa_user:
                    VE.h.wait_ge(VE.sem, tok_xa_user)
                    VE.wait(SC, tok_xpbf_user)
                VE.op(nc.vector.tensor_scalar(xa[:], x_res[:, t, :], stat_m[:, t:t + 1],
                                              stat_r[:, t:t + 1], ALU.subtract, ALU.mult))
                t_xa = VE.mark()
                VE.h.wait_ge(VE.sem, t_xa)
                VE.op(nc.vector.tensor_scalar_mul(xp[:], xa[:, 0:1024], hpre[:, t, 0:1]))
                VE.mark()
                VE.h.wait_ge(VE.sem, VE.n)
                for n in range(1, 4):
                    VE.op(nc.vector.scalar_tensor_tensor(
                        xp[:], xa[:, 1024 * n:1024 * (n + 1)], hpre[:, t, n:n + 1],
                        xp[:], ALU.mult, ALU.add))
                    VE.mark()
                    VE.h.wait_ge(VE.sem, VE.n)
                SC.wait(VE, VE.n)
                if tok_xpT_user:
                    SC.wait(PE, tok_xpT_user)
                SC.op(nc.scalar.copy(xp_bf[:], xp[:]))
                t_xpb = SC.mark()
                tok_xpbf_user = t_xpb
                PE.wait(SC, t_xpb)
                pst = ps_tf0 if (t % 2 == 0) else ps_tf1
                if tA_ev[t % 2]:
                    PE.wait(SC, tA_ev[t % 2])
                for c in range(8):
                    PE.op(nc.tensor.transpose(pst[:, c, :],
                                              xp_bf[:, 128 * c:128 * (c + 1)],
                                              ident[:]))
                t_tr = PE.mark()
                SC.wait(PE, t_tr)
                SC.op(nc.scalar.copy(xpT[:], pst[:, :, :]))
                tA_ev[t % 2] = SC.mark()
                tok_xpT_user = PE.n
                PE.wait(SC, tA_ev[0])
                PE.wait(SC, tA_ev[1])
                if tok_psy_sc:
                    PE.wait(SC, tok_psy_sc)
                for c in range(8):
                    PE.op(nc.tensor.matmul(ps_y[:, 0:512], xpT[:, c, :],
                                           wbr_s[:, c, 0:512],
                                           start=(c == 0), stop=(c == 7)))
                    PE.op(nc.tensor.matmul(ps_y[:, 512:1024], xpT[:, c, :],
                                           wbr_s[:, c, 512:1024],
                                           start=(c == 0), stop=(c == 7)))
                t_ym = PE.mark()
                tok_xpT_user = t_ym
                SC.wait(PE, t_ym)
                if tok_y_user:
                    SC.wait(VE, tok_y_user)
                SC.op(nc.scalar.copy(y_f32[:], ps_y[:]))
                t_ye = SC.mark()
                tok_psy_sc = t_ye
                VE.wait(SC, t_ye)
                chain_tok = [0, 0, 0, 0]
                for jn in range(5):
                    for i in range(4):
                        dst = x_res[:, t, 1024 * i:1024 * (i + 1)]
                        if chain_tok[i]:
                            VE.h.wait_ge(VE.sem, chain_tok[i])
                        if jn < 4:
                            VE.op(nc.vector.scalar_tensor_tensor(
                                dst, xa[:, 1024 * jn:1024 * (jn + 1)],
                                hres[:, t, 4 * i + jn:4 * i + jn + 1], dst,
                                ALU.mult, ALU.add))
                        else:
                            VE.op(nc.vector.scalar_tensor_tensor(
                                dst, y_f32[:], hpost[:, t, i:i + 1], dst,
                                ALU.mult, ALU.add))
                        chain_tok[i] = VE.mark()
                tok_y_user = VE.n
                tok_xa_user = VE.n
                if store_out:
                    SP.wait(VE, VE.n)
                    st_toks.append(dST.start(t, y[128 * t:128 * (t + 1), :],
                                             x_res[:, t, :]))
            pf.close()
            return st_toks

        if BUILD_PHASES >= 4:
            emit_ffn(store_out=False)
            barrier()
            if DEBUG:
                SP.h.wait_ge(VE.sem, VE.n)
                dST.start(1, taps["t_x2"][:], x_res[:])
                tap_quiesce()

        if BUILD_PHASES >= 5:
            p3 = ExitStack()
            sb3 = lambda nm, shp, dt=F32: p3.enter_context(nc.sbuf_tensor(nm, shp, dt))
            ln2T = sb3("ln2T", [128, NF, T], BF16)
            pb = ExitStack()
            lnb = pb.enter_context(nc.sbuf_tensor("lnb3", [128, D], BF16))
            bn63 = pb.enter_context(nc.sbuf_tensor("bn63", [128, 8, 6], F32))
            bns3 = pb.enter_context(nc.sbuf_tensor("bns3", [128, 2], F32))
            r3 = pb.enter_context(nc.sbuf_tensor("r3", [128, 1], F32))
            mb3 = pb.enter_context(nc.sbuf_tensor("mb3", [128, 1], F32))
            psT3 = [pb.enter_context(nc.psum_tensor(f"ps_t3{k}", [128, 8, 128], BF16))
                    for k in range(4)]
            tok_ev3 = [(None, 0)] * 4
            tok_bn3_ve = 0
            tok_bn3_sc = 0
            tok_lnb_user = 0
            for t in range(NT):
                if tok_bn3_ve:
                    VE.h.wait_ge(VE.sem, tok_bn3_ve)
                if tok_bn3_sc:
                    VE.wait(SC, tok_bn3_sc)
                for g8 in range(8):
                    VE.op(nc.vector.bn_stats(bn63[:, g8, :],
                                             x_res[:, t, 512 * g8:512 * (g8 + 1)]))
                t_bn = VE.mark()
                VE.h.wait_ge(VE.sem, t_bn)
                VE.op(nc.vector.bn_aggr(bns3[:], bn63[:]))
                t_ag = VE.mark()
                SC.wait(VE, t_ag)
                SC.op(nc.scalar.activation(r3[:], bns3[:, 1:2], AF.Ln, bias=eps1[:]))
                t_l = SC.mark()
                SC.h.wait_ge(SC.sem, t_l)
                SC.op(nc.scalar.activation(r3[:], r3[:], AF.Exp, scale=-0.5))
                t_r = SC.mark()
                VE.wait(SC, t_r)
                VE.op(nc.vector.tensor_scalar(mb3[:], bns3[:, 0:1], r3[:], -1.0,
                                              ALU.mult, ALU.mult))
                t_mb = VE.mark()
                tok_bn3_ve = t_mb
                SC.wait(VE, t_mb)
                if tok_lnb_user:
                    SC.wait(PE, tok_lnb_user)
                SC.op(nc.scalar.activation(lnb[:], x_res[:, t, :], AF.Identity,
                                           bias=mb3[:], scale=r3[:]))
                t_lnb = SC.mark()
                tok_bn3_sc = t_lnb
                PE.wait(SC, t_lnb)
                for fg in range(4):
                    rr = (t * 4 + fg) % 4
                    pst = psT3[rr]
                    ev_eng = SC if (fg % 2 == 0) else VE
                    e_, t_ = tok_ev3[rr]
                    if e_ is not None:
                        PE.wait(e_, t_)
                    for ff in range(8):
                        f = 8 * fg + ff
                        PE.op(nc.tensor.transpose(pst[:, ff, :],
                                                  lnb[:, 128 * f:128 * (f + 1)],
                                                  ident[:]))
                    t_tr = PE.mark()
                    ev_eng.wait(PE, t_tr)
                    if ev_eng is SC:
                        ev_eng.op(nc.scalar.copy(
                            ln2T[:, 8 * fg:8 * fg + 8, 128 * t:128 * (t + 1)],
                            pst[:, :, :]))
                    else:
                        ev_eng.op(nc.vector.tensor_copy(
                            ln2T[:, 8 * fg:8 * fg + 8, 128 * t:128 * (t + 1)],
                            pst[:, :, :]))
                    tok_ev3[rr] = (ev_eng, ev_eng.mark())
                tok_lnb_user = PE.n
            barrier()
            pb.close()

            ps_b = p3.enter_context(nc.psum_tensor("ps_b", [128, NT, 512], F32))
            wb_bf = sb3("wb_bf", [128, 2, 2, 512], BF16)
            brow = sb3("brow", [1, 2, 512], BF16)
            tok_wb = [None, None]
            tok_wb_user = [0, 0]
            tok_br = [None, None]
            tok_br_user = [0, 0]
            tok_psb_user = [0] * NT
            for os_ in range(8):
                bsl = os_ % 2
                if tok_br_user[bsl]:
                    gB.qh.wait_ge(PE.sem, tok_br_user[bsl])
                tok_br[bsl] = gB.start(bsl, brow[:, bsl, :],
                                       biasb[:, 512 * os_:512 * (os_ + 1)])
                for fg in range(16):
                    sl = (os_ * 16 + fg) % 2
                    if tok_wb_user[sl]:
                        gWO.qh.wait_ge(PE.sem, tok_wb_user[sl])
                    if os_ == 0 and fg == 0:
                        gWO.qh.wait_ge(gPC.sems[tk_pc_wb[0]], tk_pc_wb[1])
                    tok_wb[sl] = gWO.start(sl, wb_bf[:, sl, :, :],
                                           wbt_bf_d[256 * fg:256 * (fg + 1),
                                                    512 * os_:512 * (os_ + 1)]
                                           .rearrange("(a p) n -> p a n", p=128))
                    gWO.wait(PE, tok_wb[sl])
                    for ff in range(2):
                        f = 2 * fg + ff
                        for t in range(NT):
                            st = (f == 0)
                            if st and tok_psb_user[t]:
                                PE.wait(VE, tok_psb_user[t])
                            PE.op(nc.tensor.matmul(ps_b[:, t, :],
                                                   ln2T[:, f, 128 * t:128 * (t + 1)],
                                                   wb_bf[:, sl, ff, :],
                                                   start=st, stop=False))
                    tok_wb_user[sl] = PE.mark()
                gB.wait(PE, tok_br[bsl])
                for t in range(NT):
                    PE.op(nc.tensor.matmul(ps_b[:, t, :], onesb[:], brow[:, bsl, :],
                                           start=False, stop=True))
                t_bd = PE.mark()
                tok_br_user[bsl] = t_bd
                for t in range(NT):
                    VE.wait(PE, t_bd)
                    VE.op(nc.vector.tensor_tensor(
                        x_res[:, t, 512 * os_:512 * (os_ + 1)],
                        x_res[:, t, 512 * os_:512 * (os_ + 1)],
                        ps_b[:, t, :], ALU.add))
                    tok_psb_user[t] = VE.mark()
            barrier()
            p3.close()
            if DEBUG:
                SP.h.wait_ge(VE.sem, VE.n)
                dST.start(0, taps["t_x3"][:], x_res[:])
                tap_quiesce()

        st_toks = []
        if BUILD_PHASES >= 6:
            st_toks = emit_ffn(store_out=True)

        for s_, c_ in zip(dST.sems, dST.cnt):
            if c_:
                SP.h.wait_ge(s_, c_)
        if DEBUG:
            for s_, c_ in zip(gWO.sems, gWO.cnt):
                if c_:
                    SP.h.wait_ge(s_, c_)
        barrier()
        for st_ in (p1a, pa, p1):
            st_.close()

    @blk.gpsimd
    def _(_g):
        emit()

    top.close()
    return nc


def kernel(L0_attn_a_hidden_states, L0_attn_a_Wqkv, L0_attn_a_Wo,
           L0_attn_b_weight, L0_attn_b_bias,
           L0_ffn_W_res, L0_ffn_W_pre, L0_ffn_W_post,
           L0_ffn_H_res_base, L0_ffn_H_pre_base, L0_ffn_H_post_base,
           L0_ffn_gate, L0_ffn_W_branch):
    x = np.asarray(L0_attn_a_hidden_states, dtype=np.float32)
    wqkv = np.ascontiguousarray(np.asarray(L0_attn_a_Wqkv, dtype=np.float32))
    wo = np.ascontiguousarray(np.asarray(L0_attn_a_Wo, dtype=np.float32))
    wbt = np.ascontiguousarray(np.asarray(L0_attn_b_weight, dtype=np.float32).T)
    biasb = np.ascontiguousarray(np.asarray(L0_attn_b_bias, dtype=np.float32)[None, :])
    wrppt = np.ascontiguousarray(np.concatenate([
        np.asarray(L0_ffn_W_res, dtype=np.float32),
        np.asarray(L0_ffn_W_pre, dtype=np.float32),
        np.asarray(L0_ffn_W_post, dtype=np.float32)], axis=0).T)
    basespp = np.ascontiguousarray(np.broadcast_to(np.concatenate([
        np.asarray(L0_ffn_H_res_base, dtype=np.float32).reshape(-1),
        np.asarray(L0_ffn_H_pre_base, dtype=np.float32),
        np.asarray(L0_ffn_H_post_base, dtype=np.float32)]), (128, 24)))
    gate = np.asarray(L0_ffn_gate, dtype=np.float32).reshape(1, 1)
    wbr = np.ascontiguousarray(np.asarray(L0_ffn_W_branch, dtype=np.float32))

    cols = np.arange(WD)
    prow = np.arange(128)
    in_maps = []
    for c in range(8):
        b, j = c // 4, c % 4
        chunks = [j + 4 * i for i in range(NT)]
        xq_ = np.ascontiguousarray(
            np.concatenate([x[b, 128 * qc:128 * (qc + 1)] for qc in chunks], axis=0))
        Dm = (prow[:, None] - (cols[None, :] - C0) - 128 * j).astype(np.float32)
        dm = np.where(Dm <= 0, Dm, -60000.0).astype(np.float16).astype(np.float32)
        in_maps.append({
            "xk": np.ascontiguousarray(x[b]),
            "xq": xq_,
            "wqkv": wqkv, "wo": wo, "wbt": wbt, "biasb": biasb,
            "wrppt": wrppt, "basespp": basespp, "gate": gate, "wbr": wbr,
            "dmask": dm,
        })

    nc = build_program()
    res = run_bass_kernel_spmd(nc, in_maps, core_ids=list(range(8)))

    out = np.empty((2, S, D), dtype=np.float32)
    for c in range(8):
        b, j = c // 4, c % 4
        yc = np.asarray(res.results[c]["y"])
        for i in range(NT):
            qc = j + 4 * i
            out[b, 128 * qc:128 * (qc + 1)] = yc[128 * i:128 * (i + 1)]
    if DEBUG:
        kernel.last_results = res
    return out


# version 08
# speedup vs baseline: 1.0268x; 1.0268x over previous
import math
import numpy as np
import concourse.bass as bass
import concourse.mybir as mybir
from concourse.bass_utils import run_bass_kernel_spmd
from contextlib import ExitStack

F32 = mybir.dt.float32
BF16 = mybir.dt.bfloat16
AF = mybir.ActivationFunctionType
ALU = mybir.AluOpType
AX = mybir.AxisListType
F16 = mybir.dt.float16

DEBUG = False
BUILD_PHASES = 99
P1A_GROUPS = 8
P1A_OWN = True

D = 4096
NF = 32
NH = 32
HD = 128
S = 4096
NKC = 32
T = 1024
NT = 8
NEG = -1e30
C0 = 3968
WD = 8064
LN_EPS = 1e-5
RMS_EPS = 1.1920929e-07
TAU_INV = 10.0
SINK_ITERS = 50

SLOPES = [2.0 ** (-0.25 * (h + 1)) for h in range(NH)]
NW = [min(NKC, math.ceil((208.0 / s) / 128.0) + 5) for s in SLOPES]


def _i_range(kc, nw):
    i_lo = max(0, -(-(kc - 3) // 4))
    i_hi = min(NT - 1, (kc + nw - 4) // 4)
    return i_lo, i_hi


def _start_kc(i, nw):
    return max(0, 4 * i + 4 - nw)


class Eng:
    def __init__(self, nc, handle, name, stack):
        self.h = handle
        self.name = name
        self.sem = stack.enter_context(nc.semaphore("s" + name))
        self.n = 0
        self.last = None
        self.marked = True

    def op(self, inst):
        self.last = inst
        self.marked = False
        return inst

    def mark(self):
        if not self.marked:
            self.last.then_inc(self.sem, 1)
            self.n += 1
            self.marked = True
        return self.n

    def wait(self, other, tok):
        if tok > 0:
            self.h.wait_ge(other.sem, tok)


class Dma:

    def __init__(self, nc, qh, name, stack, nslots=2):
        self.qh = qh
        self.sems = [stack.enter_context(nc.semaphore(f"d{name}{i}"))
                     for i in range(nslots)]
        self.cnt = [0] * nslots
        self.nslots = nslots

    def start(self, slot, out_ap, in_ap):
        s = slot % self.nslots
        self.qh.dma_start(out_ap, in_ap).then_inc(self.sems[s], 16)
        self.cnt[s] += 16
        return (s, self.cnt[s])

    def wait(self, eng, tok):
        s, v = tok
        eng.h.wait_ge(self.sems[s], v)


def build_program():
    nc = bass.Bass("TRN2", target_bir_lowering=False)

    xk = nc.dram_tensor("xk", [S, D], F32, kind="ExternalInput").ap()
    xq = nc.dram_tensor("xq", [T, D], F32, kind="ExternalInput").ap()
    wqkv = nc.dram_tensor("wqkv", [D, 4352], F32, kind="ExternalInput").ap()
    wo = nc.dram_tensor("wo", [D, D], F32, kind="ExternalInput").ap()
    wbt = nc.dram_tensor("wbt", [D, D], F32, kind="ExternalInput").ap()
    biasb = nc.dram_tensor("biasb", [1, D], F32, kind="ExternalInput").ap()
    wrppt = nc.dram_tensor("wrppt", [D, 24], F32, kind="ExternalInput").ap()
    basespp = nc.dram_tensor("basespp", [128, 24], F32, kind="ExternalInput").ap()
    gate = nc.dram_tensor("gate", [1, 1], F32, kind="ExternalInput").ap()
    wbr = nc.dram_tensor("wbr", [1024, 1024], F32, kind="ExternalInput").ap()
    dmask = nc.dram_tensor("dmask", [128, WD], F32, kind="ExternalInput").ap()
    y = nc.dram_tensor("y", [T, D], F32, kind="ExternalOutput").ap()
    wo_bf_d = nc.dram_tensor("wo_bf_d", [D, D], BF16, kind="Internal").ap()
    wbt_bf_d = nc.dram_tensor("wbt_bf_d", [D, D], BF16, kind="Internal").ap()
    wq_bf_d = nc.dram_tensor("wq_bf_d", [D, D], BF16, kind="Internal").ap()
    taps = {}
    if DEBUG:
        taps["t_kfm"] = nc.dram_tensor("t_kfm", [128, S], F32, kind="ExternalOutput").ap()
        taps["t_v"] = nc.dram_tensor("t_v", [128, NKC, 129], F32, kind="ExternalOutput").ap()
        taps["t_lnT"] = nc.dram_tensor("t_lnT", [128, NF, T], F32, kind="ExternalOutput").ap()
        taps["t_attn"] = nc.dram_tensor("t_attn", [128, NT, D], F32, kind="ExternalOutput").ap()
        taps["t_x1"] = nc.dram_tensor("t_x1", [128, NT, D], F32, kind="ExternalOutput").ap()
        taps["t_x2"] = nc.dram_tensor("t_x2", [128, NT, D], F32, kind="ExternalOutput").ap()
        taps["t_x3"] = nc.dram_tensor("t_x3", [128, NT, D], F32, kind="ExternalOutput").ap()

    top = ExitStack()
    sb = lambda nm, shp, dt=F32: top.enter_context(nc.sbuf_tensor(nm, shp, dt))

    PE = Eng(nc, nc.tensor, "PE", top)
    VE = Eng(nc, nc.vector, "VE", top)
    SC = Eng(nc, nc.scalar, "SC", top)
    PL = Eng(nc, nc.gpsimd, "PL", top)
    SP = Eng(nc, nc.sync, "SP", top)
    ALL = [PE, VE, SC, PL, SP]

    dXK = Dma(nc, nc.sync, "xk", top)
    dXQ = Dma(nc, nc.sync, "xq", top)
    dCN = Dma(nc, nc.sync, "cn", top, nslots=1)
    dST = Dma(nc, nc.sync, "st", top)
    gW = Dma(nc, nc.gpsimd, "gw", top)
    sWQ = Dma(nc, nc.sync, "wq", top)
    gWO = Dma(nc, nc.sync, "go", top)
    gB = Dma(nc, nc.gpsimd, "gb", top)
    gPC = Dma(nc, nc.gpsimd, "pc", top, nslots=4)

    def barrier():
        toks = [(e, e.mark()) for e in (PE, VE, SC, PL)]
        for e in ALL:
            for o, t in toks:
                if o is not e and t > 0:
                    e.h.wait_ge(o.sem, t)

    ident = sb("ident", [128, 128], BF16)
    bspp = sb("bspp", [128, 24])
    wrpp_bf = sb("wrpp_bf", [128, NF, 24], BF16)
    gcol = sb("gcol", [128, 1])
    g1col = sb("g1col", [128, 1])
    onesb = sb("onesb", [1, 128], BF16)
    onesf = sb("onesf", [1, 128])
    eps1 = sb("eps1", [128, 1])
    epsr = sb("epsr", [128, 1])
    gtmp = sb("gtmp", [1, 1])

    blk = top.enter_context(nc.Block())

    def emit():
        p1 = ExitStack()
        p1a = ExitStack()
        sb1 = lambda nm, shp, dt=F32: p1.enter_context(nc.sbuf_tensor(nm, shp, dt))
        sb1a = lambda nm, shp, dt=F32: p1a.enter_context(nc.sbuf_tensor(nm, shp, dt))
        lnT_own = sb1("lnT_own", [128, NF, T], BF16)
        kfm = sb1("kfm", [128, S], BF16)
        vtiles = sb1("vtiles", [128, NKC, 129], BF16)
        pa = ExitStack()


        def tap_quiesce():
            for dmaq in (dST, gWO, gPC):
                for s_, c_ in zip(dmaq.sems, dmaq.cnt):
                    if c_:
                        for e_ in ALL:
                            e_.h.wait_ge(s_, c_)

        PL.op(nc.gpsimd.memset(ident[:], 1.0))
        t_ms = PL.mark()
        PL.h.wait_ge(PL.sem, t_ms)
        PL.op(nc.gpsimd.affine_select(out=ident[:], in_=ident[:],
                                      compare_op=ALU.is_equal, fill=0.0, base=0,
                                      pattern=[[-1, 128]], channel_multiplier=1))
        PL.op(nc.gpsimd.memset(onesb[:], 1.0))
        PL.op(nc.gpsimd.memset(onesf[:], 1.0))
        PL.op(nc.gpsimd.memset(eps1[:], LN_EPS))
        PL.op(nc.gpsimd.memset(epsr[:], RMS_EPS))
        PL.op(nc.gpsimd.memset(vtiles[:, :, 128], 1.0))
        t_p0pl = PL.mark()

        dCN.start(0, bspp[:], basespp[:])
        tk_cn = dCN.start(0, gtmp[:], gate[:])

        wkv_bf = sb1a("wkv_bf", [128, NF, 256], BF16)
        gW.start(0, wkv_bf[:], wqkv[:, 4096:4352].rearrange("(c p) n -> p c n", p=128))
        tk_w0 = gW.start(0, wrpp_bf[:], wrppt[:].rearrange("(c p) n -> p c n", p=128))
        for sl4 in range(4):
            gPC.start(0, wq_bf_d[1024 * sl4:1024 * (sl4 + 1), :],
                      wqkv[1024 * sl4:1024 * (sl4 + 1), 0:4096])
        for sl4 in range(4):
            gPC.start(1, wo_bf_d[1024 * sl4:1024 * (sl4 + 1), :],
                      wo[1024 * sl4:1024 * (sl4 + 1), :])
        for sl4 in range(4):
            gPC.start(2, wbt_bf_d[1024 * sl4:1024 * (sl4 + 1), :],
                      wbt[1024 * sl4:1024 * (sl4 + 1), :])
        tk_pc_q = (0, 4 * 16)
        tk_pc_wo = (1, 4 * 16)
        tk_pc_wb = (2, 4 * 16)

        ps_g = p1a.enter_context(nc.psum_tensor("ps_g", [128, 1], F32))
        dCN.wait(PE, tk_cn)
        PE.wait(PL, t_p0pl)
        PE.op(nc.tensor.matmul(ps_g[:], onesf[:], gtmp[:], start=True, stop=True))
        t_gbc = PE.mark()
        SC.wait(PE, t_gbc)
        SC.wait(PL, t_p0pl)
        SC.op(nc.scalar.activation(gcol[:], ps_g[:], AF.Exp, scale=-1.0))
        t_ge = SC.mark()
        VE.wait(SC, t_ge)
        VE.op(nc.vector.tensor_scalar_add(gcol[:], gcol[:], 1.0))
        t_g1 = VE.mark()
        VE.h.wait_ge(VE.sem, t_g1)
        VE.op(nc.vector.reciprocal(gcol[:], gcol[:]))
        t_g2 = VE.mark()
        VE.h.wait_ge(VE.sem, t_g2)
        VE.op(nc.vector.tensor_scalar(g1col[:], gcol[:], -1.0, 1.0, ALU.mult, ALU.add))
        t_g3 = VE.mark()

        xt = sb1a("xt", [128, 2, D])
        ln_bf = sb1a("ln_bf", [128, 2, D], BF16)
        lnT4 = sb1a("lnT4", [128, NF, 512], BF16)
        bn6 = sb1a("bn6", [128, 2, 8, 6])
        bnst = sb1a("bnst", [128, 2, 2])
        rcol = sb1a("rcol", [128, 2])
        mbcol = sb1a("mbcol", [128, 2])
        vfm_bf = sb1a("vfm_bf", [128, 512], BF16)
        ps_k = p1a.enter_context(nc.psum_tensor("ps_k", [128, 512], F32))
        ps_v = p1a.enter_context(nc.psum_tensor("ps_v", [128, 512], F32))
        psT = [p1a.enter_context(nc.psum_tensor(f"ps_t{k}", [128, 8, 128], BF16))
               for k in range(4)]

        state = {
            "xt_user": [[], []],
            "lnbf_user": [0, 0],
            "pst_evac": [0, 0, 0, 0],
            "pst_rr": [0],
            "stats_user": [0, 0],
        }

        def ln_tile_pipeline(src_ap, par, dma, dtok, lnT_dst, dst_free_tok=0,
                             ve_apply=False):
            dma.wait(VE, dtok)
            if state["stats_user"][par]:
                e_, t_ = state["stats_user"][par]
                VE.wait(e_, t_) if e_ is not VE else VE.h.wait_ge(VE.sem, t_)
            for g8 in range(8):
                VE.op(nc.vector.bn_stats(bn6[:, par, g8, :],
                                         src_ap[:, 512 * g8:512 * (g8 + 1)]))
            t_bn = VE.mark()
            VE.h.wait_ge(VE.sem, t_bn)
            VE.op(nc.vector.bn_aggr(bnst[:, par, :], bn6[:, par, :, :]))
            t_ag = VE.mark()
            SC.wait(VE, t_ag)
            SC.op(nc.scalar.activation(rcol[:, par:par + 1], bnst[:, par, 1:2],
                                       AF.Ln, bias=eps1[:]))
            t_l = SC.mark()
            SC.h.wait_ge(SC.sem, t_l)
            SC.op(nc.scalar.activation(rcol[:, par:par + 1], rcol[:, par:par + 1],
                                       AF.Exp, scale=-0.5))
            t_r = SC.mark()
            if ve_apply:
                VE.wait(SC, t_r)
                if state["lnbf_user"][par]:
                    VE.wait(PE, state["lnbf_user"][par])
                VE.op(nc.vector.tensor_scalar(ln_bf[:, par, :], src_ap,
                                              bnst[:, par, 0:1], rcol[:, par:par + 1],
                                              ALU.subtract, ALU.mult))
                t_ap = VE.mark()
                state["stats_user"][par] = (VE, t_ap)
                PE.wait(VE, t_ap)
            else:
                VE.wait(SC, t_r)
                VE.op(nc.vector.tensor_scalar(mbcol[:, par:par + 1], bnst[:, par, 0:1],
                                              rcol[:, par:par + 1], -1.0,
                                              ALU.mult, ALU.mult))
                t_mb = VE.mark()
                SC.wait(VE, t_mb)
                if state["lnbf_user"][par]:
                    SC.wait(PE, state["lnbf_user"][par])
                SC.op(nc.scalar.activation(ln_bf[:, par, :], src_ap, AF.Identity,
                                           bias=mbcol[:, par:par + 1],
                                           scale=rcol[:, par:par + 1]))
                t_ap = SC.mark()
                state["stats_user"][par] = (SC, t_ap)
                PE.wait(SC, t_ap)
            for fg in range(4):
                rr = state["pst_rr"][0] % 4
                state["pst_rr"][0] += 1
                pst = psT[rr]
                ev_eng = SC if (fg % 2 == 0) else VE
                if state["pst_evac"][rr]:
                    e_, t_ = state["pst_evac"][rr]
                    PE.wait(e_, t_)
                for ff in range(8):
                    f = 8 * fg + ff
                    PE.op(nc.tensor.transpose(pst[:, ff, :],
                                              ln_bf[:, par, 128 * f:128 * (f + 1)],
                                              ident[:]))
                t_tr = PE.mark()
                ev_eng.wait(PE, t_tr)
                if fg < 2 and dst_free_tok:
                    ev_eng.wait(PE, dst_free_tok)
                if ev_eng is SC:
                    ev_eng.op(nc.scalar.copy(lnT_dst(fg), pst[:, :, :]))
                else:
                    ev_eng.op(nc.vector.tensor_copy(lnT_dst(fg), pst[:, :, :]))
                state["pst_evac"][rr] = (ev_eng, ev_eng.mark())
            state["lnbf_user"][par] = PE.n
            return (SC.n, VE.n), t_ap, t_ag

        tok_kv_prev = 0
        tok_vfm_user = 0
        if BUILD_PHASES >= 1:
            for g4 in range(P1A_GROUPS):
                last_ev = 0
                for tt in range(4):
                    t = 4 * g4 + tt
                    par = t % 2
                    for (ue, ut) in state["xt_user"][par]:
                        SP.wait(ue, ut)
                    tk = dXK.start(t, xt[:, par, :], xk[128 * t:128 * (t + 1), :])

                    def lnT_dst(fg, tt=tt):
                        return lnT4[:, 8 * fg:8 * fg + 8, 128 * tt:128 * (tt + 1)]

                    last_ev, t_ap, t_ag = ln_tile_pipeline(
                        xt[:, par, :], par, dXK, tk, lnT_dst,
                        dst_free_tok=tok_kv_prev if tt == 0 else 0,
                        ve_apply=(t % 2 == 1))
                    ap_eng = VE if (t % 2 == 1) else SC
                    state["xt_user"][par] = [(ap_eng, t_ap), (VE, t_ag)]
                PE.wait(SC, last_ev[0])
                PE.wait(VE, last_ev[1])
                gW.wait(PE, tk_w0)
                for f in range(NF):
                    PE.op(nc.tensor.matmul(ps_k[:], wkv_bf[:, f, 0:128], lnT4[:, f, :],
                                           start=(f == 0), stop=(f == NF - 1)))
                    PE.op(nc.tensor.matmul(ps_v[:], wkv_bf[:, f, 128:256], lnT4[:, f, :],
                                           start=(f == 0), stop=(f == NF - 1)))
                t_kv = PE.mark()
                tok_kv_prev = t_kv
                SC.wait(PE, t_kv)
                SC.op(nc.scalar.copy(kfm[:, 512 * g4:512 * (g4 + 1)], ps_k[:]))
                t_ke = SC.mark()
                SC.h.wait_ge(SC.sem, t_ke)
                if tok_vfm_user:
                    SC.wait(PE, tok_vfm_user)
                SC.op(nc.scalar.copy(vfm_bf[:], ps_v[:]))
                t_vev = SC.mark()
                PE.wait(SC, t_vev)
                for tt in range(4):
                    rr = state["pst_rr"][0] % 4
                    state["pst_rr"][0] += 1
                    pst = psT[rr]
                    if state["pst_evac"][rr]:
                        e_, t_ = state["pst_evac"][rr]
                        PE.wait(e_, t_)
                    PE.op(nc.tensor.transpose(pst[:, 0, :],
                                              vfm_bf[:, 128 * tt:128 * (tt + 1)], ident[:]))
                    t_tv = PE.mark()
                    SC.wait(PE, t_tv)
                    SC.op(nc.scalar.copy(vtiles[:, 4 * g4 + tt, 0:128], pst[:, 0, :]))
                    state["pst_evac"][rr] = (SC, SC.mark())
                tok_vfm_user = PE.n

            for t in range(NT if P1A_OWN else 0):
                par = t % 2
                for (ue, ut) in state["xt_user"][par]:
                    SP.wait(ue, ut)
                tk = dXQ.start(t, xt[:, par, :], xq[128 * t:128 * (t + 1), :])

                def lnT_dst(fg, t=t):
                    return lnT_own[:, 8 * fg:8 * fg + 8, 128 * t:128 * (t + 1)]

                _, t_ap, t_ag = ln_tile_pipeline(xt[:, par, :], par, dXQ, tk, lnT_dst,
                                                 ve_apply=(t % 2 == 1))
                ap_eng = VE if (t % 2 == 1) else SC
                state["xt_user"][par] = [(ap_eng, t_ap), (VE, t_ag)]

        tok_lnT_own_ready = SC.mark()
        tok_lnT_own_ready_ve = VE.mark()

        if DEBUG and BUILD_PHASES >= 1:
            barrier()
            SP.h.wait_ge(SC.sem, tok_lnT_own_ready)
            gPC.start(3, taps["t_kfm"][:], kfm[:])
            gPC.start(3, taps["t_v"][:], vtiles[:])
            gPC.start(3, taps["t_lnT"][:], lnT_own[:])
            tap_quiesce()

        if BUILD_PHASES >= 2:
            barrier()
            p1a.close()
            attn_out = pa.enter_context(
                nc.sbuf_tensor("attn_out", [128, NT, D], BF16, side="right"))
            dmask_s = sb1("dmask_s", [128, WD], F16)
            gW.qh.wait_ge(gW.sems[0], gW.cnt[0])
            tk_dm = gW.start(0, dmask_s[:], dmask[:])
            q_bf = sb1("q_bf", [128, 2, T], BF16)
            probs = sb1("probs", [128, 3, 512])
            probsT = sb1("probsT", [128, 3, 512], BF16)
            wq_bf = sb1("wq_bf", [128, 2, NF, 128], BF16)
            rec = sb1("rec", [128, 8])
            ps_q = p1.enter_context(nc.psum_tensor("ps_q", [128, 512], F32))
            ps_s = p1.enter_context(nc.psum_tensor("ps_s", [128, 3, 512], F32))
            pvt = [p1.enter_context(nc.psum_tensor(f"pv{k}", [128, 129], F32))
                   for k in range(4)]
            dmv2 = dmask_s[:].rearrange("p (a b) -> p a b", b=128)

            tok_wq = [None, None]
            tok_qevac_last = 0
            tok_qbf_user = [0, 0]
            tok_ps_s_user = [0, 0, 0]
            tok_probs_user = [0, 0, 0]
            tok_probsT_user = [0, 0, 0]
            tok_pv_user = [0] * 4
            tok_rec_user = [0]
            tok_qgen_done = 0

            sWQ.qh.wait_ge(gPC.sems[tk_pc_q[0]], tk_pc_q[1])
            tok_wq[0] = sWQ.start(0, wq_bf[:, 0, :, :],
                                  wq_bf_d[:, 0:128].rearrange("(c p) n -> p c n", p=128))
            gW.wait(VE, tk_dm)

            sjob = 0

            def emit_qgen(hh, par_, out_tok):
                acts = []
                st_ = {"first": True, "ev0": 0}
                qs = 1.0 / (SLOPES[hh] * math.sqrt(HD))

                def mk(f, half):
                    def go():
                        if st_["first"]:
                            sWQ.wait(PE, tok_wq[par_])
                            PE.wait(SC, tok_lnT_own_ready)
                            PE.wait(VE, tok_lnT_own_ready_ve)
                            if tok_qevac_last[0]:
                                PE.wait(SC, tok_qevac_last[0])
                            st_["first"] = False
                        if half == 1 and f == 0:
                            PE.wait(SC, st_["ev0"])
                        PE.op(nc.tensor.matmul(ps_q[:], wq_bf[:, par_, f, :],
                                               lnT_own[:, f, 512 * half:512 * (half + 1)],
                                               start=(f == 0), stop=(f == NF - 1)))
                        if f == NF - 1:
                            t_g = PE.mark()
                            out_tok["qgen_done"] = t_g
                            SC.wait(PE, t_g)
                            if half == 0 and tok_qbf_user[par_]:
                                SC.wait(PE, tok_qbf_user[par_])
                            SC.op(nc.scalar.activation(
                                q_bf[:, par_, 512 * half:512 * (half + 1)],
                                ps_q[:], AF.Identity, scale=qs))
                            tk = SC.mark()
                            if half == 0:
                                st_["ev0"] = tk
                            else:
                                out_tok["q_ready"] = tk
                                tok_qevac_last[0] = tk
                    return go
                for half in range(2):
                    for f in range(NF):
                        acts.append(mk(f, half))
                return acts

            tok_qevac_last = [0]

            h0_tok = {}
            for th in emit_qgen(0, 0, h0_tok):
                th()
            tok_qgen_done = h0_tok["qgen_done"]
            tok_q_cur = h0_tok["q_ready"]

            pending_wq = [1]

            def issue_wq():
                hh = pending_wq[0]
                if hh < NH:
                    sWQ.qh.wait_ge(PE.sem, tok_qgen_done)
                    tok_wq[hh % 2] = sWQ.start(hh % 2, wq_bf[:, hh % 2, :, :],
                                               wq_bf_d[:, 128 * hh:128 * (hh + 1)]
                                               .rearrange("(c p) n -> p c n", p=128))
                    pending_wq[0] = hh + 1

            issue_wq()
            for h in range(NH):
                par = h % 2
                nxt = (h + 1) % 2
                nx_tok = {}
                filler = emit_qgen(h + 1, nxt, nx_tok) if h + 1 < NH else []
                fill_i = 0

                nw = NW[h]
                slope = SLOPES[h]
                jobs = []
                for grp in range(2):
                    glo, ghi = 4 * grp, 4 * grp + 3
                    ranges = []
                    for kc in range(NKC):
                        i_lo, i_hi = _i_range(kc, nw)
                        i0, i1 = max(i_lo, glo), min(i_hi, ghi)
                        if i0 <= i1:
                            ranges.append((kc, i0, i1))
                    k = 0
                    while k < len(ranges):
                        kcs = [ranges[k][0]]
                        ri0, ri1 = ranges[k][1], ranges[k][2]
                        k2 = k + 1
                        while k2 < len(ranges) and ranges[k2][0] == kcs[-1] + 1:
                            n_i0 = min(ri0, ranges[k2][1])
                            n_i1 = max(ri1, ranges[k2][2])
                            if (len(kcs) + 1) * 128 * (n_i1 - n_i0 + 1) > 512:
                                break
                            kcs.append(ranges[k2][0])
                            ri0, ri1 = n_i0, n_i1
                            k2 += 1
                        jobs.append((list(reversed(kcs)), ri0, ri1, glo))
                        k = k2

                def emit_scores_dve_act(job, sj):
                    kcs, i0, i1, glo = job
                    nch = i1 - i0 + 1
                    rcols = 128 * nch
                    nkc = len(kcs)
                    cols = rcols * nkc
                    slot = sj % 3
                    pp = sj % 3
                    if tok_ps_s_user[slot]:
                        PE.wait(VE, tok_ps_s_user[slot])
                    for kx, kc in enumerate(kcs):
                        PE.op(nc.tensor.matmul(
                            ps_s[:, slot, rcols * kx:rcols * (kx + 1)],
                            kfm[:, 128 * kc:128 * (kc + 1)],
                            q_bf[:, par, 128 * i0:128 * i0 + rcols],
                            start=True, stop=True))
                    t_sc = PE.mark()
                    VE.wait(PE, t_sc)
                    if tok_probs_user[pp]:
                        VE.wait(SC, tok_probs_user[pp])
                    base_hi = C0 + 128 * (4 * i0 - kcs[0])
                    s0 = base_hi // 128
                    dview = dmv2[:, s0:s0 + 4 * nch] \
                        .rearrange("p (c k) b -> p c k b", k=4) \
                        .rearrange("p c k b -> p k c b")[:, 0:nkc, :, :]
                    pview = probs[:, pp, 0:cols].rearrange(
                        "p (a b c) -> p a b c", a=nkc, c=128)
                    sview = ps_s[:, slot, 0:cols].rearrange(
                        "p (a b c) -> p a b c", a=nkc, c=128)
                    VE.op(nc.vector.tensor_tensor(pview, sview, dview, ALU.add))
                    t_da = VE.mark()
                    tok_ps_s_user[slot] = t_da
                    SC.wait(VE, t_da)
                    if tok_probsT_user[pp]:
                        SC.wait(PE, tok_probsT_user[pp])
                    SC.op(nc.scalar.activation(probsT[:, pp, 0:cols],
                                               probs[:, pp, 0:cols], AF.Exp,
                                               scale=slope))
                    t_ex = SC.mark()
                    tok_probs_user[pp] = t_ex
                    return t_ex

                def emit_pv(job, sj, t_ex):
                    kcs, i0, i1, glo = job
                    nch = i1 - i0 + 1
                    rcols = 128 * nch
                    pp = sj % 3
                    PE.wait(SC, t_ex)
                    stopped = []
                    for kx in range(len(kcs) - 1, -1, -1):
                        kc = kcs[kx]
                        gi0, gi1 = _i_range(kc, nw)
                        for ii in range(nch):
                            i = i0 + ii
                            if i < max(gi0, glo) or i > min(gi1, glo + 3):
                                continue
                            st = (kc == _start_kc(i, nw))
                            spf = (kc == 4 * i + 3)
                            if spf:
                                stopped.append(i)
                            if st and tok_pv_user[i - glo]:
                                PE.wait(SC, tok_pv_user[i - glo])
                            PE.op(nc.tensor.matmul(
                                pvt[i - glo][:],
                                probsT[:, pp, rcols * kx + 128 * ii:
                                       rcols * kx + 128 * (ii + 1)],
                                vtiles[:, kc, :], start=st, stop=spf))
                    t_pvt = PE.mark()
                    tok_probsT_user[pp] = t_pvt
                    for i in stopped:
                        if True:
                            il = i - glo
                            VE.wait(PE, t_pvt)
                            if tok_rec_user[0]:
                                VE.wait(SC, tok_rec_user[0])
                            VE.op(nc.vector.reciprocal(rec[:, i:i + 1],
                                                       pvt[il][:][:, 128:129]))
                            t_rc = VE.mark()
                            SC.wait(VE, t_rc)
                            SC.op(nc.scalar.activation(
                                attn_out[:, i, 128 * h:128 * (h + 1)],
                                pvt[il][:][:, 0:128], AF.Identity,
                                scale=rec[:, i:i + 1]))
                            tok_pv_user[il] = SC.mark()
                            tok_rec_user[0] = tok_pv_user[il]

                prev = None
                first_mm = True
                for job in jobs:
                    if first_mm:
                        PE.wait(SC, tok_q_cur)
                        first_mm = False
                    t_ex = emit_scores_dve_act(job, sjob)
                    if prev is not None:
                        emit_pv(prev[0], prev[1], prev[2])
                    for _ in range(3):
                        if fill_i < len(filler):
                            filler[fill_i]()
                            fill_i += 1
                    if fill_i >= len(filler) and pending_wq[0] == h + 2:
                        tok_qgen_done = nx_tok["qgen_done"]
                        issue_wq()
                    prev = (job, sjob, t_ex)
                    sjob += 1
                if prev is not None:
                    emit_pv(prev[0], prev[1], prev[2])
                while fill_i < len(filler):
                    filler[fill_i]()
                    fill_i += 1
                tok_qbf_user[par] = PE.n
                if h + 1 < NH:
                    tok_qgen_done = nx_tok["qgen_done"]
                    if pending_wq[0] == h + 2:
                        issue_wq()
                    tok_q_cur = nx_tok["q_ready"]

        if DEBUG and BUILD_PHASES >= 2:
            barrier()
            SP.h.wait_ge(SC.sem, SC.n)
            gPC.start(3, taps["t_attn"][:], attn_out[:])
            tap_quiesce()

        x_res = None
        if BUILD_PHASES >= 3:
            barrier()
            p1.close()
            p1c = ExitStack()
            sbc = lambda nm, shp, dt=F32: p1c.enter_context(nc.sbuf_tensor(nm, shp, dt))
            attn_outT = sbc("attn_outT", [128, NF, T], BF16)
            pt = ExitStack()
            psTc = [pt.enter_context(nc.psum_tensor(f"ps_tc{k}", [128, 8, 128], BF16))
                    for k in range(4)]
            tok_ev = [(None, 0)] * 4
            rrc = 0
            for t in range(NT):
                for fg in range(4):
                    rr = rrc % 4
                    rrc += 1
                    pst = psTc[rr]
                    ev_eng = SC if (fg % 2 == 0) else VE
                    e_, t_ = tok_ev[rr]
                    if e_ is not None:
                        PE.wait(e_, t_)
                    for ff in range(8):
                        f = 8 * fg + ff
                        PE.op(nc.tensor.transpose(pst[:, ff, :],
                                                  attn_out[:, t, 128 * f:128 * (f + 1)],
                                                  ident[:]))
                    t_tr = PE.mark()
                    ev_eng.wait(PE, t_tr)
                    if ev_eng is SC:
                        ev_eng.op(nc.scalar.copy(
                            attn_outT[:, 8 * fg:8 * fg + 8, 128 * t:128 * (t + 1)],
                            pst[:, :, :]))
                    else:
                        ev_eng.op(nc.vector.tensor_copy(
                            attn_outT[:, 8 * fg:8 * fg + 8, 128 * t:128 * (t + 1)],
                            pst[:, :, :]))
                    tok_ev[rr] = (ev_eng, ev_eng.mark())
            barrier()
            pa.close()
            pt.close()
            x_res = top.enter_context(
                nc.sbuf_tensor("x_res", [128, NT, D], F32, side="right"))

            ps_wo = p1c.enter_context(nc.psum_tensor("ps_wo", [128, NT, 512], F32))
            wo_bf = sbc("wo_bf", [128, 2, 2, 512], BF16)
            xs_buf = sbc("xs_buf", [128, 2, 512])
            tok_wo = [None, None]
            tok_wo_user = [0, 0]
            tok_xs_user = [0, 0]
            tok_pswo_user = [0] * NT
            for os_ in range(8):
                for fg in range(16):
                    sl = (os_ * 16 + fg) % 2
                    if tok_wo_user[sl]:
                        gWO.qh.wait_ge(PE.sem, tok_wo_user[sl])
                    if os_ == 0 and fg == 0:
                        gWO.qh.wait_ge(gPC.sems[tk_pc_wo[0]], tk_pc_wo[1])
                    tok_wo[sl] = gWO.start(sl, wo_bf[:, sl, :, :],
                                           wo_bf_d[256 * fg:256 * (fg + 1),
                                                   512 * os_:512 * (os_ + 1)]
                                           .rearrange("(a p) n -> p a n", p=128))
                    gWO.wait(PE, tok_wo[sl])
                    for ff in range(2):
                        f = 2 * fg + ff
                        for t in range(NT):
                            st = (f == 0)
                            spf = (f == NF - 1)
                            if st and tok_pswo_user[t]:
                                PE.wait(VE, tok_pswo_user[t])
                            PE.op(nc.tensor.matmul(ps_wo[:, t, :],
                                                   attn_outT[:, f, 128 * t:128 * (t + 1)],
                                                   wo_bf[:, sl, ff, :],
                                                   start=st, stop=spf))
                    tok_wo_user[sl] = PE.mark()
                t_wod = PE.n
                for t in range(NT):
                    spar = t % 2
                    if tok_xs_user[spar]:
                        SP.wait(VE, tok_xs_user[spar])
                    tkx = dXQ.start(spar, xs_buf[:, spar, :],
                                    xq[128 * t:128 * (t + 1), 512 * os_:512 * (os_ + 1)])
                    dXQ.wait(VE, tkx)
                    VE.wait(PE, t_wod)
                    VE.op(nc.vector.tensor_tensor(
                        x_res[:, t, 512 * os_:512 * (os_ + 1)],
                        ps_wo[:, t, :], xs_buf[:, spar, :], ALU.add))
                    t_x1 = VE.mark()
                    tok_pswo_user[t] = t_x1
                    tok_xs_user[spar] = t_x1
            barrier()
            p1c.close()
            if DEBUG:
                SP.h.wait_ge(VE.sem, VE.n)
                dST.start(0, taps["t_x1"][:], x_res[:])
                tap_quiesce()

        def emit_ffn(store_out):
            tag = "b" if store_out else "a"
            pf = ExitStack()
            sbf = lambda nm, shp, dt=F32: pf.enter_context(
                nc.sbuf_tensor(nm + tag, shp, dt))
            xa = sbf("xa", [128, D])
            xa_bf = sbf("xa_bf", [128, D], BF16)
            xaT = sbf("xaT", [128, NF, 128], BF16)
            Lbuf = sbf("Lbuf", [128, NT, 24])
            stat_m = sbf("stat_m", [128, NT])
            stat_v = sbf("stat_v", [128, NT])
            stat_r = sbf("stat_r", [128, NT])
            bn6f = sbf("bn6f", [128, 8, 6])
            bnsf = sbf("bnsf", [128, 2])
            mbf = sbf("mbf", [128, 1])
            rmsc = sbf("rmsc", [128, 1])
            Z = sbf("Z", [128, NT, 4, 4])
            ub = sbf("ub", [128, NT, 4])
            vb = sbf("vb", [128, NT, 4])
            Tb = sbf("Tb", [128, NT, 4, 4])
            Kp = sbf("Kp", [128, NT, 4, 4])
            Mb = sbf("Mb", [128, NT, 4])
            Sb = sbf("Sb", [128, NT, 4])
            Lb = sbf("Lb", [128, NT, 4])
            smax = sbf("smax", [128, NT])
            hpre = sbf("hpre", [128, NT, 4])
            hpost = sbf("hpost", [128, NT, 4])
            hres = sbf("hres", [128, NT, 16])
            xp = sbf("xp", [128, 1024])
            xp_bf = sbf("xp_bf", [128, 1024], BF16)
            xpT = sbf("xpT", [128, 8, 128], BF16)
            y_f32 = sbf("y_f32", [128, 1024])
            wbr_s = sbf("wbr_s", [128, 8, 1024], BF16)
            gW.qh.wait_ge(gW.sems[1], gW.cnt[1])
            tk_wbr = gW.start(1, wbr_s[:], wbr[:].rearrange("(c p) n -> p c n", p=128))
            ps_L = pf.enter_context(nc.psum_tensor("ps_L" + tag, [128, 24], F32))
            ps_tf0 = pf.enter_context(nc.psum_tensor("ps_tf0" + tag, [128, 8, 128], BF16))
            ps_tf1 = pf.enter_context(nc.psum_tensor("ps_tf1" + tag, [128, 8, 128], BF16))
            ps_y = pf.enter_context(nc.psum_tensor("ps_y" + tag, [128, 1024], F32))

            tA_ev = [0, 0]
            tok_bn_user_ve = 0
            tok_bn_user_sc = 0
            tok_xabf_user = 0
            tok_psL_user = 0
            tok_rms_user = 0
            for t in range(NT):
                if tok_bn_user_ve:
                    VE.h.wait_ge(VE.sem, tok_bn_user_ve)
                if tok_bn_user_sc:
                    VE.wait(SC, tok_bn_user_sc)
                for g8 in range(8):
                    VE.op(nc.vector.bn_stats(bn6f[:, g8, :],
                                             x_res[:, t, 512 * g8:512 * (g8 + 1)]))
                t_bn = VE.mark()
                VE.h.wait_ge(VE.sem, t_bn)
                VE.op(nc.vector.bn_aggr(bnsf[:], bn6f[:]))
                t_ag = VE.mark()
                VE.h.wait_ge(VE.sem, t_ag)
                VE.op(nc.vector.tensor_copy(stat_m[:, t:t + 1], bnsf[:, 0:1]))
                VE.op(nc.vector.tensor_copy(stat_v[:, t:t + 1], bnsf[:, 1:2]))
                t_cp = VE.mark()
                tok_bn_user_ve = t_cp
                SC.wait(VE, t_cp)
                SC.op(nc.scalar.activation(stat_r[:, t:t + 1], stat_v[:, t:t + 1],
                                           AF.Ln, bias=eps1[:]))
                t_l1 = SC.mark()
                SC.h.wait_ge(SC.sem, t_l1)
                SC.op(nc.scalar.activation(stat_r[:, t:t + 1], stat_r[:, t:t + 1],
                                           AF.Exp, scale=-0.5))
                t_r = SC.mark()
                tok_bn_user_sc = t_r
                VE.wait(SC, t_r)
                if tok_rms_user:
                    VE.wait(SC, tok_rms_user)
                VE.op(nc.vector.tensor_scalar(rmsc[:], stat_v[:, t:t + 1],
                                              stat_r[:, t:t + 1], stat_r[:, t:t + 1],
                                              ALU.mult, ALU.mult))
                t_r2 = VE.mark()
                SC.wait(VE, t_r2)
                SC.op(nc.scalar.activation(rmsc[:], rmsc[:], AF.Ln, bias=epsr[:]))
                t_l2 = SC.mark()
                SC.h.wait_ge(SC.sem, t_l2)
                SC.op(nc.scalar.activation(rmsc[:], rmsc[:], AF.Exp, scale=-0.5))
                t_rms = SC.mark()
                VE.wait(SC, t_r)
                VE.op(nc.vector.tensor_scalar(mbf[:], stat_m[:, t:t + 1],
                                              stat_r[:, t:t + 1], -1.0,
                                              ALU.mult, ALU.mult))
                t_mb = VE.mark()
                SC.wait(VE, t_mb)
                if tok_xabf_user:
                    SC.wait(PE, tok_xabf_user)
                SC.op(nc.scalar.activation(xa_bf[:], x_res[:, t, :], AF.Identity,
                                           bias=mbf[:], scale=stat_r[:, t:t + 1]))
                t_xa = SC.mark()
                PE.wait(SC, t_xa)
                for fg in range(4):
                    pp = fg % 2
                    pst = ps_tf0 if pp == 0 else ps_tf1
                    if tA_ev[pp]:
                        PE.wait(SC, tA_ev[pp])
                    for ff in range(8):
                        f = 8 * fg + ff
                        PE.op(nc.tensor.transpose(pst[:, ff, :],
                                                  xa_bf[:, 128 * f:128 * (f + 1)],
                                                  ident[:]))
                    t_tr = PE.mark()
                    SC.wait(PE, t_tr)
                    SC.op(nc.scalar.copy(xaT[:, 8 * fg:8 * fg + 8, :], pst[:, :, :]))
                    tA_ev[pp] = SC.mark()
                tok_xabf_user = PE.n
                PE.wait(SC, tA_ev[0])
                PE.wait(SC, tA_ev[1])
                gW.wait(PE, tk_w0)
                if tok_psL_user:
                    PE.wait(SC, tok_psL_user)
                for f in range(NF):
                    PE.op(nc.tensor.matmul(ps_L[:], xaT[:, f, :], wrpp_bf[:, f, :],
                                           start=(f == 0), stop=(f == NF - 1)))
                t_lg = PE.mark()
                SC.wait(PE, t_lg)
                SC.h.wait_ge(SC.sem, t_rms)
                SC.op(nc.scalar.activation(Lbuf[:, t, :], ps_L[:], AF.Identity,
                                           scale=rmsc[:]))
                t_le = SC.mark()
                tok_psL_user = t_le
                tok_rms_user = t_le
                dCN.wait(VE, tk_cn)
                VE.wait(SC, t_le)
                VE.op(nc.vector.tensor_tensor(Lbuf[:, t, :], Lbuf[:, t, :], bspp[:],
                                              ALU.add))
                VE.mark()

            t_A = VE.n
            VE.h.wait_ge(VE.sem, t_A)
            last_exp_sc = 0
            for (lo, dst) in ((16, hpre), (20, hpost)):
                VE.h.wait_ge(VE.sem, VE.n)
                Lv = Lbuf[:, :, lo:lo + 4]
                VE.op(nc.vector.reduce_max(
                    smax[:].rearrange("p (a o) -> p a o", o=1), Lv, AX.X))
                t_m = VE.mark()
                VE.h.wait_ge(VE.sem, t_m)
                smb = smax[:].rearrange("p (a o) -> p a o", o=1).broadcast_to([128, NT, 4])
                VE.op(nc.vector.tensor_sub(dst[:], Lv, smb))
                t_s = VE.mark()
                SC.wait(VE, t_s)
                SC.op(nc.scalar.activation(dst[:], dst[:], AF.Exp))
                t_e = SC.mark()
                last_exp_sc = t_e
                VE.wait(SC, t_e)
                VE.op(nc.vector.reduce_sum(
                    smax[:].rearrange("p (a o) -> p a o", o=1), dst[:], AX.X))
                t_ss = VE.mark()
                VE.h.wait_ge(VE.sem, t_ss)
                VE.op(nc.vector.reciprocal(smax[:], smax[:]))
                t_rcp = VE.mark()
                VE.h.wait_ge(VE.sem, t_rcp)
                VE.op(nc.vector.tensor_mul(dst[:], dst[:], smb))
                VE.mark()
            VE.op(nc.vector.tensor_scalar_mul(
                Z[:].rearrange("p a i j -> p a (i j)"), Lbuf[:, :, 0:16], TAU_INV))
            t_z = VE.mark()
            VE.h.wait_ge(VE.sem, t_z)
            PL.op(nc.gpsimd.memset(ub[:], 0.0))
            PL.op(nc.gpsimd.memset(vb[:], 0.0))
            t_z0 = PL.mark()
            VE.wait(PL, t_z0)

            st_sink = {"t4": last_exp_sc}

            def half_iter(z_view, pot_in, pot_out):
                if st_sink["t4"]:
                    VE.wait(SC, st_sink["t4"])
                pib = pot_in[:].rearrange("p a (o i) -> p a o i", o=1) \
                    .broadcast_to([128, NT, 4, 4])
                VE.op(nc.vector.tensor_sub(Tb[:], z_view, pib))
                t1 = VE.mark()
                VE.h.wait_ge(VE.sem, t1)
                VE.op(nc.vector.reduce_max(Mb[:], Tb[:], AX.X))
                t2 = VE.mark()
                VE.h.wait_ge(VE.sem, t2)
                mb2 = Mb[:].rearrange("p a (i o) -> p a i o", o=1) \
                    .broadcast_to([128, NT, 4, 4])
                VE.op(nc.vector.tensor_sub(Tb[:], Tb[:], mb2))
                t3 = VE.mark()
                SC.wait(VE, t3)
                SC.op(nc.scalar.activation(Tb[:], Tb[:], AF.Exp))
                t4 = SC.mark()
                st_sink["t4"] = t4
                VE.wait(SC, t4)
                VE.op(nc.vector.reduce_sum(Sb[:], Tb[:], AX.X))
                t5 = VE.mark()
                SC.wait(VE, t5)
                SC.op(nc.scalar.activation(Lb[:], Sb[:], AF.Ln))
                t6 = SC.mark()
                VE.wait(SC, t6)
                VE.op(nc.vector.tensor_add(pot_out[:], Lb[:], Mb[:]))
                t7 = VE.mark()
                VE.h.wait_ge(VE.sem, t7)

            zv = Z[:]
            zvT = Z[:].rearrange("p a i j -> p a j i")
            half_iter(zv, vb, ub)
            half_iter(zvT, ub, vb)
            VE.wait(SC, st_sink["t4"])
            uib = ub[:].rearrange("p a (i o) -> p a i o", o=1).broadcast_to([128, NT, 4, 4])
            vjb = vb[:].rearrange("p a (o i) -> p a o i", o=1).broadcast_to([128, NT, 4, 4])
            VE.op(nc.vector.tensor_sub(Tb[:], Z[:], uib))
            t_k1 = VE.mark(); VE.h.wait_ge(VE.sem, t_k1)
            VE.op(nc.vector.tensor_sub(Tb[:], Tb[:], vjb))
            t_k2 = VE.mark()
            SC.wait(VE, t_k2)
            SC.op(nc.scalar.activation(Kp[:], Tb[:], AF.Exp))
            t_kp = SC.mark()
            VE.wait(SC, t_kp)
            VE.op(nc.vector.memset(ub[:], 1.0))
            VE.op(nc.vector.memset(vb[:], 1.0))
            t_ab0 = VE.mark(); VE.h.wait_ge(VE.sem, t_ab0)
            KpT = Kp[:].rearrange("p a i j -> p a j i")
            for _ in range(SINK_ITERS - 1):
                bb = vb[:].rearrange("p a (o i) -> p a o i", o=1) \
                    .broadcast_to([128, NT, 4, 4])
                VE.op(nc.vector.tensor_mul(Tb[:], Kp[:], bb))
                t1 = VE.mark(); VE.h.wait_ge(VE.sem, t1)
                VE.op(nc.vector.reduce_sum(Sb[:], Tb[:], AX.X))
                t2 = VE.mark(); VE.h.wait_ge(VE.sem, t2)
                VE.op(nc.vector.reciprocal(ub[:], Sb[:]))
                t3 = VE.mark(); VE.h.wait_ge(VE.sem, t3)
                ab = ub[:].rearrange("p a (o i) -> p a o i", o=1) \
                    .broadcast_to([128, NT, 4, 4])
                VE.op(nc.vector.tensor_mul(Tb[:], KpT, ab))
                t4 = VE.mark(); VE.h.wait_ge(VE.sem, t4)
                VE.op(nc.vector.reduce_sum(Sb[:], Tb[:], AX.X))
                t5 = VE.mark(); VE.h.wait_ge(VE.sem, t5)
                VE.op(nc.vector.reciprocal(vb[:], Sb[:]))
                t6 = VE.mark(); VE.h.wait_ge(VE.sem, t6)
            bb = vb[:].rearrange("p a (o i) -> p a o i", o=1).broadcast_to([128, NT, 4, 4])
            VE.op(nc.vector.tensor_mul(Tb[:], Kp[:], bb))
            t_h1 = VE.mark(); VE.h.wait_ge(VE.sem, t_h1)
            aib = ub[:].rearrange("p a (i o) -> p a i o", o=1).broadcast_to([128, NT, 4, 4])
            VE.op(nc.vector.tensor_mul(
                hres[:].rearrange("p a (i j) -> p a i j", j=4), Tb[:], aib))
            t_h4 = VE.mark(); VE.h.wait_ge(VE.sem, t_h4)
            VE.wait(VE, t_g3)
            VE.op(nc.vector.tensor_scalar_mul(hres[:], hres[:], gcol[:]))
            t_h5 = VE.mark(); VE.h.wait_ge(VE.sem, t_h5)
            for dd in range(4):
                VE.op(nc.vector.tensor_scalar_add(hres[:, :, 5 * dd:5 * dd + 1],
                                                  hres[:, :, 5 * dd:5 * dd + 1],
                                                  g1col[:]))
            t_hres = VE.mark()

            gW.wait(PE, tk_wbr)
            tok_xa_user = 0
            tok_y_user = 0
            tok_psy_sc = 0
            tok_xpbf_user = 0
            tok_xpT_user = 0
            st_toks = []
            for t in range(NT):
                VE.h.wait_ge(VE.sem, t_hres)
                if tok_xa_user:
                    VE.h.wait_ge(VE.sem, tok_xa_user)
                    VE.wait(SC, tok_xpbf_user)
                VE.op(nc.vector.tensor_scalar(xa[:], x_res[:, t, :], stat_m[:, t:t + 1],
                                              stat_r[:, t:t + 1], ALU.subtract, ALU.mult))
                t_xa = VE.mark()
                VE.h.wait_ge(VE.sem, t_xa)
                VE.op(nc.vector.tensor_scalar_mul(xp[:], xa[:, 0:1024], hpre[:, t, 0:1]))
                VE.mark()
                VE.h.wait_ge(VE.sem, VE.n)
                for n in range(1, 4):
                    VE.op(nc.vector.scalar_tensor_tensor(
                        xp[:], xa[:, 1024 * n:1024 * (n + 1)], hpre[:, t, n:n + 1],
                        xp[:], ALU.mult, ALU.add))
                    VE.mark()
                    VE.h.wait_ge(VE.sem, VE.n)
                SC.wait(VE, VE.n)
                if tok_xpT_user:
                    SC.wait(PE, tok_xpT_user)
                SC.op(nc.scalar.copy(xp_bf[:], xp[:]))
                t_xpb = SC.mark()
                tok_xpbf_user = t_xpb
                PE.wait(SC, t_xpb)
                pst = ps_tf0 if (t % 2 == 0) else ps_tf1
                if tA_ev[t % 2]:
                    PE.wait(SC, tA_ev[t % 2])
                for c in range(8):
                    PE.op(nc.tensor.transpose(pst[:, c, :],
                                              xp_bf[:, 128 * c:128 * (c + 1)],
                                              ident[:]))
                t_tr = PE.mark()
                SC.wait(PE, t_tr)
                SC.op(nc.scalar.copy(xpT[:], pst[:, :, :]))
                tA_ev[t % 2] = SC.mark()
                tok_xpT_user = PE.n
                PE.wait(SC, tA_ev[0])
                PE.wait(SC, tA_ev[1])
                if tok_psy_sc:
                    PE.wait(SC, tok_psy_sc)
                for c in range(8):
                    PE.op(nc.tensor.matmul(ps_y[:, 0:512], xpT[:, c, :],
                                           wbr_s[:, c, 0:512],
                                           start=(c == 0), stop=(c == 7)))
                    PE.op(nc.tensor.matmul(ps_y[:, 512:1024], xpT[:, c, :],
                                           wbr_s[:, c, 512:1024],
                                           start=(c == 0), stop=(c == 7)))
                t_ym = PE.mark()
                tok_xpT_user = t_ym
                SC.wait(PE, t_ym)
                if tok_y_user:
                    SC.wait(VE, tok_y_user)
                SC.op(nc.scalar.copy(y_f32[:], ps_y[:]))
                t_ye = SC.mark()
                tok_psy_sc = t_ye
                VE.wait(SC, t_ye)
                chain_tok = [0, 0, 0, 0]
                for jn in range(5):
                    for i in range(4):
                        dst = x_res[:, t, 1024 * i:1024 * (i + 1)]
                        if chain_tok[i]:
                            VE.h.wait_ge(VE.sem, chain_tok[i])
                        if jn < 4:
                            VE.op(nc.vector.scalar_tensor_tensor(
                                dst, xa[:, 1024 * jn:1024 * (jn + 1)],
                                hres[:, t, 4 * i + jn:4 * i + jn + 1], dst,
                                ALU.mult, ALU.add))
                        else:
                            VE.op(nc.vector.scalar_tensor_tensor(
                                dst, y_f32[:], hpost[:, t, i:i + 1], dst,
                                ALU.mult, ALU.add))
                        chain_tok[i] = VE.mark()
                tok_y_user = VE.n
                tok_xa_user = VE.n
                if store_out:
                    SP.wait(VE, VE.n)
                    st_toks.append(dST.start(t, y[128 * t:128 * (t + 1), :],
                                             x_res[:, t, :]))
            pf.close()
            return st_toks

        if BUILD_PHASES >= 4:
            emit_ffn(store_out=False)
            barrier()
            if DEBUG:
                SP.h.wait_ge(VE.sem, VE.n)
                dST.start(1, taps["t_x2"][:], x_res[:])
                tap_quiesce()

        if BUILD_PHASES >= 5:
            p3 = ExitStack()
            sb3 = lambda nm, shp, dt=F32: p3.enter_context(nc.sbuf_tensor(nm, shp, dt))
            ln2T = sb3("ln2T", [128, NF, T], BF16)
            pb = ExitStack()
            lnb = pb.enter_context(nc.sbuf_tensor("lnb3", [128, D], BF16))
            bn63 = pb.enter_context(nc.sbuf_tensor("bn63", [128, 8, 6], F32))
            bns3 = pb.enter_context(nc.sbuf_tensor("bns3", [128, 2], F32))
            r3 = pb.enter_context(nc.sbuf_tensor("r3", [128, 1], F32))
            mb3 = pb.enter_context(nc.sbuf_tensor("mb3", [128, 1], F32))
            psT3 = [pb.enter_context(nc.psum_tensor(f"ps_t3{k}", [128, 8, 128], BF16))
                    for k in range(4)]
            tok_ev3 = [(None, 0)] * 4
            tok_bn3_ve = 0
            tok_bn3_sc = 0
            tok_lnb_user = 0
            for t in range(NT):
                if tok_bn3_ve:
                    VE.h.wait_ge(VE.sem, tok_bn3_ve)
                if tok_bn3_sc:
                    VE.wait(SC, tok_bn3_sc)
                for g8 in range(8):
                    VE.op(nc.vector.bn_stats(bn63[:, g8, :],
                                             x_res[:, t, 512 * g8:512 * (g8 + 1)]))
                t_bn = VE.mark()
                VE.h.wait_ge(VE.sem, t_bn)
                VE.op(nc.vector.bn_aggr(bns3[:], bn63[:]))
                t_ag = VE.mark()
                SC.wait(VE, t_ag)
                SC.op(nc.scalar.activation(r3[:], bns3[:, 1:2], AF.Ln, bias=eps1[:]))
                t_l = SC.mark()
                SC.h.wait_ge(SC.sem, t_l)
                SC.op(nc.scalar.activation(r3[:], r3[:], AF.Exp, scale=-0.5))
                t_r = SC.mark()
                VE.wait(SC, t_r)
                VE.op(nc.vector.tensor_scalar(mb3[:], bns3[:, 0:1], r3[:], -1.0,
                                              ALU.mult, ALU.mult))
                t_mb = VE.mark()
                tok_bn3_ve = t_mb
                SC.wait(VE, t_mb)
                if tok_lnb_user:
                    SC.wait(PE, tok_lnb_user)
                SC.op(nc.scalar.activation(lnb[:], x_res[:, t, :], AF.Identity,
                                           bias=mb3[:], scale=r3[:]))
                t_lnb = SC.mark()
                tok_bn3_sc = t_lnb
                PE.wait(SC, t_lnb)
                for fg in range(4):
                    rr = (t * 4 + fg) % 4
                    pst = psT3[rr]
                    ev_eng = SC if (fg % 2 == 0) else VE
                    e_, t_ = tok_ev3[rr]
                    if e_ is not None:
                        PE.wait(e_, t_)
                    for ff in range(8):
                        f = 8 * fg + ff
                        PE.op(nc.tensor.transpose(pst[:, ff, :],
                                                  lnb[:, 128 * f:128 * (f + 1)],
                                                  ident[:]))
                    t_tr = PE.mark()
                    ev_eng.wait(PE, t_tr)
                    if ev_eng is SC:
                        ev_eng.op(nc.scalar.copy(
                            ln2T[:, 8 * fg:8 * fg + 8, 128 * t:128 * (t + 1)],
                            pst[:, :, :]))
                    else:
                        ev_eng.op(nc.vector.tensor_copy(
                            ln2T[:, 8 * fg:8 * fg + 8, 128 * t:128 * (t + 1)],
                            pst[:, :, :]))
                    tok_ev3[rr] = (ev_eng, ev_eng.mark())
                tok_lnb_user = PE.n
            barrier()
            pb.close()

            ps_b = p3.enter_context(nc.psum_tensor("ps_b", [128, NT, 512], F32))
            wb_bf = sb3("wb_bf", [128, 2, 2, 512], BF16)
            brow = sb3("brow", [1, 2, 512], BF16)
            tok_wb = [None, None]
            tok_wb_user = [0, 0]
            tok_br = [None, None]
            tok_br_user = [0, 0]
            tok_psb_user = [0] * NT
            for os_ in range(8):
                bsl = os_ % 2
                if tok_br_user[bsl]:
                    gB.qh.wait_ge(PE.sem, tok_br_user[bsl])
                tok_br[bsl] = gB.start(bsl, brow[:, bsl, :],
                                       biasb[:, 512 * os_:512 * (os_ + 1)])
                for fg in range(16):
                    sl = (os_ * 16 + fg) % 2
                    if tok_wb_user[sl]:
                        gWO.qh.wait_ge(PE.sem, tok_wb_user[sl])
                    if os_ == 0 and fg == 0:
                        gWO.qh.wait_ge(gPC.sems[tk_pc_wb[0]], tk_pc_wb[1])
                    tok_wb[sl] = gWO.start(sl, wb_bf[:, sl, :, :],
                                           wbt_bf_d[256 * fg:256 * (fg + 1),
                                                    512 * os_:512 * (os_ + 1)]
                                           .rearrange("(a p) n -> p a n", p=128))
                    gWO.wait(PE, tok_wb[sl])
                    for ff in range(2):
                        f = 2 * fg + ff
                        for t in range(NT):
                            st = (f == 0)
                            if st and tok_psb_user[t]:
                                PE.wait(VE, tok_psb_user[t])
                            PE.op(nc.tensor.matmul(ps_b[:, t, :],
                                                   ln2T[:, f, 128 * t:128 * (t + 1)],
                                                   wb_bf[:, sl, ff, :],
                                                   start=st, stop=False))
                    tok_wb_user[sl] = PE.mark()
                gB.wait(PE, tok_br[bsl])
                for t in range(NT):
                    PE.op(nc.tensor.matmul(ps_b[:, t, :], onesb[:], brow[:, bsl, :],
                                           start=False, stop=True))
                t_bd = PE.mark()
                tok_br_user[bsl] = t_bd
                for t in range(NT):
                    VE.wait(PE, t_bd)
                    VE.op(nc.vector.tensor_tensor(
                        x_res[:, t, 512 * os_:512 * (os_ + 1)],
                        x_res[:, t, 512 * os_:512 * (os_ + 1)],
                        ps_b[:, t, :], ALU.add))
                    tok_psb_user[t] = VE.mark()
            barrier()
            p3.close()
            if DEBUG:
                SP.h.wait_ge(VE.sem, VE.n)
                dST.start(0, taps["t_x3"][:], x_res[:])
                tap_quiesce()

        st_toks = []
        if BUILD_PHASES >= 6:
            st_toks = emit_ffn(store_out=True)

        for s_, c_ in zip(dST.sems, dST.cnt):
            if c_:
                SP.h.wait_ge(s_, c_)
        if DEBUG:
            for s_, c_ in zip(gWO.sems, gWO.cnt):
                if c_:
                    SP.h.wait_ge(s_, c_)
        barrier()
        for st_ in (p1a, pa, p1):
            st_.close()

    @blk.gpsimd
    def _(_g):
        emit()

    top.close()
    return nc


def kernel(L0_attn_a_hidden_states, L0_attn_a_Wqkv, L0_attn_a_Wo,
           L0_attn_b_weight, L0_attn_b_bias,
           L0_ffn_W_res, L0_ffn_W_pre, L0_ffn_W_post,
           L0_ffn_H_res_base, L0_ffn_H_pre_base, L0_ffn_H_post_base,
           L0_ffn_gate, L0_ffn_W_branch):
    x = np.asarray(L0_attn_a_hidden_states, dtype=np.float32)
    wqkv = np.ascontiguousarray(np.asarray(L0_attn_a_Wqkv, dtype=np.float32))
    wo = np.ascontiguousarray(np.asarray(L0_attn_a_Wo, dtype=np.float32))
    wbt = np.ascontiguousarray(np.asarray(L0_attn_b_weight, dtype=np.float32).T)
    biasb = np.ascontiguousarray(np.asarray(L0_attn_b_bias, dtype=np.float32)[None, :])
    wrppt = np.ascontiguousarray(np.concatenate([
        np.asarray(L0_ffn_W_res, dtype=np.float32),
        np.asarray(L0_ffn_W_pre, dtype=np.float32),
        np.asarray(L0_ffn_W_post, dtype=np.float32)], axis=0).T)
    basespp = np.ascontiguousarray(np.broadcast_to(np.concatenate([
        np.asarray(L0_ffn_H_res_base, dtype=np.float32).reshape(-1),
        np.asarray(L0_ffn_H_pre_base, dtype=np.float32),
        np.asarray(L0_ffn_H_post_base, dtype=np.float32)]), (128, 24)))
    gate = np.asarray(L0_ffn_gate, dtype=np.float32).reshape(1, 1)
    wbr = np.ascontiguousarray(np.asarray(L0_ffn_W_branch, dtype=np.float32))

    cols = np.arange(WD)
    prow = np.arange(128)
    in_maps = []
    for c in range(8):
        b, j = c // 4, c % 4
        chunks = [j + 4 * i for i in range(NT)]
        xq_ = np.ascontiguousarray(
            np.concatenate([x[b, 128 * qc:128 * (qc + 1)] for qc in chunks], axis=0))
        Dm = (prow[:, None] - (cols[None, :] - C0) - 128 * j).astype(np.float32)
        dm = np.where(Dm <= 0, Dm, -60000.0).astype(np.float16).astype(np.float32)
        in_maps.append({
            "xk": np.ascontiguousarray(x[b]),
            "xq": xq_,
            "wqkv": wqkv, "wo": wo, "wbt": wbt, "biasb": biasb,
            "wrppt": wrppt, "basespp": basespp, "gate": gate, "wbr": wbr,
            "dmask": dm,
        })

    nc = build_program()
    res = run_bass_kernel_spmd(nc, in_maps, core_ids=list(range(8)))

    out = np.empty((2, S, D), dtype=np.float32)
    for c in range(8):
        b, j = c // 4, c % 4
        yc = np.asarray(res.results[c]["y"])
        for i in range(NT):
            qc = j + 4 * i
            out[b, 128 * qc:128 * (qc + 1)] = yc[128 * i:128 * (i + 1)]
    if DEBUG:
        kernel.last_results = res
    return out


# version 09
# speedup vs baseline: 1.1631x; 1.1327x over previous
import math
import numpy as np
import concourse.bass as bass
import concourse.mybir as mybir
from concourse.bass_utils import run_bass_kernel_spmd
from contextlib import ExitStack

F32 = mybir.dt.float32
BF16 = mybir.dt.bfloat16
AF = mybir.ActivationFunctionType
ALU = mybir.AluOpType
AX = mybir.AxisListType
F16 = mybir.dt.float16

DEBUG = False
BUILD_PHASES = 99
P1A_GROUPS = 8
P1A_OWN = True

D = 4096
NF = 32
NH = 32
HD = 128
S = 4096
NKC = 32
T = 1024
NT = 8
NEG = -1e30
C0 = 3968
WD = 8064
LN_EPS = 1e-5
RMS_EPS = 1.1920929e-07
TAU_INV = 10.0
SINK_ITERS = 50

SLOPES = [2.0 ** (-0.25 * (h + 1)) for h in range(NH)]
NW = [min(NKC, math.ceil((208.0 / s) / 128.0) + 5) for s in SLOPES]


def _i_range(kc, nw):
    i_lo = max(0, -(-(kc - 3) // 4))
    i_hi = min(NT - 1, (kc + nw - 4) // 4)
    return i_lo, i_hi


def _start_kc(i, nw):
    return max(0, 4 * i + 4 - nw)


class Eng:
    def __init__(self, nc, handle, name, stack):
        self.h = handle
        self.name = name
        self.sem = stack.enter_context(nc.semaphore("s" + name))
        self.n = 0
        self.last = None
        self.marked = True

    def op(self, inst):
        self.last = inst
        self.marked = False
        return inst

    def mark(self):
        if not self.marked:
            self.last.then_inc(self.sem, 1)
            self.n += 1
            self.marked = True
        return self.n

    def wait(self, other, tok):
        if tok > 0:
            self.h.wait_ge(other.sem, tok)


class Dma:

    def __init__(self, nc, qh, name, stack, nslots=2):
        self.qh = qh
        self.sems = [stack.enter_context(nc.semaphore(f"d{name}{i}"))
                     for i in range(nslots)]
        self.cnt = [0] * nslots
        self.nslots = nslots

    def start(self, slot, out_ap, in_ap):
        s = slot % self.nslots
        self.qh.dma_start(out_ap, in_ap).then_inc(self.sems[s], 16)
        self.cnt[s] += 16
        return (s, self.cnt[s])

    def wait(self, eng, tok):
        s, v = tok
        eng.h.wait_ge(self.sems[s], v)


def build_program():
    nc = bass.Bass("TRN2", target_bir_lowering=False)

    xk = nc.dram_tensor("xk", [S, D], F32, kind="ExternalInput").ap()
    xq = nc.dram_tensor("xq", [T, D], F32, kind="ExternalInput").ap()
    wqkv = nc.dram_tensor("wqkv", [D, 4352], F32, kind="ExternalInput").ap()
    wo = nc.dram_tensor("wo", [D, D], F32, kind="ExternalInput").ap()
    wbt = nc.dram_tensor("wbt", [D, D], F32, kind="ExternalInput").ap()
    biasb = nc.dram_tensor("biasb", [1, D], F32, kind="ExternalInput").ap()
    wrppt = nc.dram_tensor("wrppt", [D, 24], F32, kind="ExternalInput").ap()
    basespp = nc.dram_tensor("basespp", [128, 24], F32, kind="ExternalInput").ap()
    gate = nc.dram_tensor("gate", [1, 1], F32, kind="ExternalInput").ap()
    wbr = nc.dram_tensor("wbr", [1024, 1024], F32, kind="ExternalInput").ap()
    dmask = nc.dram_tensor("dmask", [128, WD], F32, kind="ExternalInput").ap()
    y = nc.dram_tensor("y", [T, D], F32, kind="ExternalOutput").ap()
    wo_bf_d = nc.dram_tensor("wo_bf_d", [D, D], BF16, kind="Internal").ap()
    wbt_bf_d = nc.dram_tensor("wbt_bf_d", [D, D], BF16, kind="Internal").ap()
    wq_bf_d = nc.dram_tensor("wq_bf_d", [D, D], BF16, kind="Internal").ap()
    taps = {}
    if DEBUG:
        taps["t_kfm"] = nc.dram_tensor("t_kfm", [128, S], F32, kind="ExternalOutput").ap()
        taps["t_v"] = nc.dram_tensor("t_v", [128, NKC, 129], F32, kind="ExternalOutput").ap()
        taps["t_lnT"] = nc.dram_tensor("t_lnT", [128, NF, T], F32, kind="ExternalOutput").ap()
        taps["t_attn"] = nc.dram_tensor("t_attn", [128, NT, D], F32, kind="ExternalOutput").ap()
        taps["t_x1"] = nc.dram_tensor("t_x1", [128, NT, D], F32, kind="ExternalOutput").ap()
        taps["t_x2"] = nc.dram_tensor("t_x2", [128, NT, D], F32, kind="ExternalOutput").ap()
        taps["t_x3"] = nc.dram_tensor("t_x3", [128, NT, D], F32, kind="ExternalOutput").ap()

    top = ExitStack()
    sb = lambda nm, shp, dt=F32: top.enter_context(nc.sbuf_tensor(nm, shp, dt))

    PE = Eng(nc, nc.tensor, "PE", top)
    VE = Eng(nc, nc.vector, "VE", top)
    SC = Eng(nc, nc.scalar, "SC", top)
    PL = Eng(nc, nc.gpsimd, "PL", top)
    SP = Eng(nc, nc.sync, "SP", top)
    ALL = [PE, VE, SC, PL, SP]

    dXK = Dma(nc, nc.sync, "xk", top)
    dXQ = Dma(nc, nc.sync, "xq", top)
    dCN = Dma(nc, nc.sync, "cn", top, nslots=1)
    dST = Dma(nc, nc.sync, "st", top)
    gW = Dma(nc, nc.gpsimd, "gw", top)
    sWQ = Dma(nc, nc.sync, "wq", top)
    gWO = Dma(nc, nc.sync, "go", top)
    gB = Dma(nc, nc.gpsimd, "gb", top)
    gPC = Dma(nc, nc.gpsimd, "pc", top, nslots=4)

    def barrier():
        toks = [(e, e.mark()) for e in (PE, VE, SC, PL)]
        for e in ALL:
            for o, t in toks:
                if o is not e and t > 0:
                    e.h.wait_ge(o.sem, t)

    ident = sb("ident", [128, 128], BF16)
    bspp = sb("bspp", [128, 24])
    wrpp_bf = sb("wrpp_bf", [128, NF, 24], BF16)
    gcol = sb("gcol", [128, 1])
    g1col = sb("g1col", [128, 1])
    onesb = sb("onesb", [1, 128], BF16)
    onesf = sb("onesf", [1, 128])
    eps1 = sb("eps1", [128, 1])
    epsr = sb("epsr", [128, 1])
    gtmp = sb("gtmp", [1, 1])

    blk = top.enter_context(nc.Block())

    def emit():
        p1 = ExitStack()
        p1a = ExitStack()
        sb1 = lambda nm, shp, dt=F32: p1.enter_context(nc.sbuf_tensor(nm, shp, dt))
        sb1a = lambda nm, shp, dt=F32: p1a.enter_context(nc.sbuf_tensor(nm, shp, dt))
        lnT_own = sb1("lnT_own", [128, NF, T], BF16)
        kfm = sb1("kfm", [128, S], BF16)
        vtiles = sb1("vtiles", [128, NKC, 129], BF16)
        pa = ExitStack()


        def tap_quiesce():
            for dmaq in (dST, gWO, gPC):
                for s_, c_ in zip(dmaq.sems, dmaq.cnt):
                    if c_:
                        for e_ in ALL:
                            e_.h.wait_ge(s_, c_)

        PL.op(nc.gpsimd.memset(ident[:], 1.0))
        t_ms = PL.mark()
        PL.h.wait_ge(PL.sem, t_ms)
        PL.op(nc.gpsimd.affine_select(out=ident[:], in_=ident[:],
                                      compare_op=ALU.is_equal, fill=0.0, base=0,
                                      pattern=[[-1, 128]], channel_multiplier=1))
        PL.op(nc.gpsimd.memset(onesb[:], 1.0))
        PL.op(nc.gpsimd.memset(onesf[:], 1.0))
        PL.op(nc.gpsimd.memset(eps1[:], LN_EPS))
        PL.op(nc.gpsimd.memset(epsr[:], RMS_EPS))
        PL.op(nc.gpsimd.memset(vtiles[:, :, 128], 1.0))
        t_p0pl = PL.mark()

        dCN.start(0, bspp[:], basespp[:])
        tk_cn = dCN.start(0, gtmp[:], gate[:])

        wkv_bf = sb1a("wkv_bf", [128, NF, 256], BF16)
        gW.start(0, wkv_bf[:], wqkv[:, 4096:4352].rearrange("(c p) n -> p c n", p=128))
        tk_w0 = gW.start(0, wrpp_bf[:], wrppt[:].rearrange("(c p) n -> p c n", p=128))
        for sl4 in range(4):
            gPC.start(0, wq_bf_d[1024 * sl4:1024 * (sl4 + 1), :],
                      wqkv[1024 * sl4:1024 * (sl4 + 1), 0:4096])
        for sl4 in range(4):
            gPC.start(1, wo_bf_d[1024 * sl4:1024 * (sl4 + 1), :],
                      wo[1024 * sl4:1024 * (sl4 + 1), :])
        for sl4 in range(4):
            gPC.start(2, wbt_bf_d[1024 * sl4:1024 * (sl4 + 1), :],
                      wbt[1024 * sl4:1024 * (sl4 + 1), :])
        tk_pc_q = (0, 4 * 16)
        tk_pc_wo = (1, 4 * 16)
        tk_pc_wb = (2, 4 * 16)

        ps_g = p1a.enter_context(nc.psum_tensor("ps_g", [128, 1], F32))
        dCN.wait(PE, tk_cn)
        PE.wait(PL, t_p0pl)
        PE.op(nc.tensor.matmul(ps_g[:], onesf[:], gtmp[:], start=True, stop=True))
        t_gbc = PE.mark()
        SC.wait(PE, t_gbc)
        SC.wait(PL, t_p0pl)
        SC.op(nc.scalar.activation(gcol[:], ps_g[:], AF.Exp, scale=-1.0))
        t_ge = SC.mark()
        VE.wait(SC, t_ge)
        VE.op(nc.vector.tensor_scalar_add(gcol[:], gcol[:], 1.0))
        t_g1 = VE.mark()
        VE.h.wait_ge(VE.sem, t_g1)
        VE.op(nc.vector.reciprocal(gcol[:], gcol[:]))
        t_g2 = VE.mark()
        VE.h.wait_ge(VE.sem, t_g2)
        VE.op(nc.vector.tensor_scalar(g1col[:], gcol[:], -1.0, 1.0, ALU.mult, ALU.add))
        t_g3 = VE.mark()

        xt = sb1a("xt", [128, 2, D])
        ln_bf = sb1a("ln_bf", [128, 2, D], BF16)
        lnT4 = sb1a("lnT4", [128, NF, 512], BF16)
        bn6 = sb1a("bn6", [128, 2, 8, 6])
        bnst = sb1a("bnst", [128, 2, 2])
        rcol = sb1a("rcol", [128, 2])
        mbcol = sb1a("mbcol", [128, 2])
        vfm_bf = sb1a("vfm_bf", [128, 512], BF16)
        ps_k = p1a.enter_context(nc.psum_tensor("ps_k", [128, 512], F32))
        ps_v = p1a.enter_context(nc.psum_tensor("ps_v", [128, 512], F32))
        psT = [p1a.enter_context(nc.psum_tensor(f"ps_t{k}", [128, 8, 128], BF16))
               for k in range(4)]

        state = {
            "xt_user": [[], []],
            "lnbf_user": [0, 0],
            "pst_evac": [0, 0, 0, 0],
            "pst_rr": [0],
            "stats_user": [0, 0],
        }

        def ln_tile_pipeline(src_ap, par, dma, dtok, lnT_dst, dst_free_tok=0,
                             ve_apply=False):
            dma.wait(VE, dtok)
            if state["stats_user"][par]:
                e_, t_ = state["stats_user"][par]
                VE.wait(e_, t_) if e_ is not VE else VE.h.wait_ge(VE.sem, t_)
            for g8 in range(8):
                VE.op(nc.vector.bn_stats(bn6[:, par, g8, :],
                                         src_ap[:, 512 * g8:512 * (g8 + 1)]))
            t_bn = VE.mark()
            VE.h.wait_ge(VE.sem, t_bn)
            VE.op(nc.vector.bn_aggr(bnst[:, par, :], bn6[:, par, :, :]))
            t_ag = VE.mark()
            SC.wait(VE, t_ag)
            SC.op(nc.scalar.activation(rcol[:, par:par + 1], bnst[:, par, 1:2],
                                       AF.Ln, bias=eps1[:]))
            t_l = SC.mark()
            SC.h.wait_ge(SC.sem, t_l)
            SC.op(nc.scalar.activation(rcol[:, par:par + 1], rcol[:, par:par + 1],
                                       AF.Exp, scale=-0.5))
            t_r = SC.mark()
            if ve_apply:
                VE.wait(SC, t_r)
                if state["lnbf_user"][par]:
                    VE.wait(PE, state["lnbf_user"][par])
                VE.op(nc.vector.tensor_scalar(ln_bf[:, par, :], src_ap,
                                              bnst[:, par, 0:1], rcol[:, par:par + 1],
                                              ALU.subtract, ALU.mult))
                t_ap = VE.mark()
                state["stats_user"][par] = (VE, t_ap)
                PE.wait(VE, t_ap)
            else:
                VE.wait(SC, t_r)
                VE.op(nc.vector.tensor_scalar(mbcol[:, par:par + 1], bnst[:, par, 0:1],
                                              rcol[:, par:par + 1], -1.0,
                                              ALU.mult, ALU.mult))
                t_mb = VE.mark()
                SC.wait(VE, t_mb)
                if state["lnbf_user"][par]:
                    SC.wait(PE, state["lnbf_user"][par])
                SC.op(nc.scalar.activation(ln_bf[:, par, :], src_ap, AF.Identity,
                                           bias=mbcol[:, par:par + 1],
                                           scale=rcol[:, par:par + 1]))
                t_ap = SC.mark()
                state["stats_user"][par] = (SC, t_ap)
                PE.wait(SC, t_ap)
            for fg in range(4):
                rr = state["pst_rr"][0] % 4
                state["pst_rr"][0] += 1
                pst = psT[rr]
                ev_eng = SC if (fg % 2 == 0) else VE
                if state["pst_evac"][rr]:
                    e_, t_ = state["pst_evac"][rr]
                    PE.wait(e_, t_)
                for ff in range(8):
                    f = 8 * fg + ff
                    PE.op(nc.tensor.transpose(pst[:, ff, :],
                                              ln_bf[:, par, 128 * f:128 * (f + 1)],
                                              ident[:]))
                t_tr = PE.mark()
                ev_eng.wait(PE, t_tr)
                if fg < 2 and dst_free_tok:
                    ev_eng.wait(PE, dst_free_tok)
                if ev_eng is SC:
                    ev_eng.op(nc.scalar.copy(lnT_dst(fg), pst[:, :, :]))
                else:
                    ev_eng.op(nc.vector.tensor_copy(lnT_dst(fg), pst[:, :, :]))
                state["pst_evac"][rr] = (ev_eng, ev_eng.mark())
            state["lnbf_user"][par] = PE.n
            return (SC.n, VE.n), t_ap, t_ag

        tok_kv_prev = 0
        tok_vfm_user = 0
        if BUILD_PHASES >= 1:
            for g4 in range(P1A_GROUPS):
                last_ev = 0
                for tt in range(4):
                    t = 4 * g4 + tt
                    par = t % 2
                    for (ue, ut) in state["xt_user"][par]:
                        SP.wait(ue, ut)
                    tk = dXK.start(t, xt[:, par, :], xk[128 * t:128 * (t + 1), :])

                    def lnT_dst(fg, tt=tt):
                        return lnT4[:, 8 * fg:8 * fg + 8, 128 * tt:128 * (tt + 1)]

                    last_ev, t_ap, t_ag = ln_tile_pipeline(
                        xt[:, par, :], par, dXK, tk, lnT_dst,
                        dst_free_tok=tok_kv_prev if tt == 0 else 0)
                    state["xt_user"][par] = [(SC, t_ap), (VE, t_ag)]
                PE.wait(SC, last_ev[0])
                PE.wait(VE, last_ev[1])
                gW.wait(PE, tk_w0)
                for f in range(NF):
                    PE.op(nc.tensor.matmul(ps_k[:], wkv_bf[:, f, 0:128], lnT4[:, f, :],
                                           start=(f == 0), stop=(f == NF - 1)))
                    PE.op(nc.tensor.matmul(ps_v[:], wkv_bf[:, f, 128:256], lnT4[:, f, :],
                                           start=(f == 0), stop=(f == NF - 1)))
                t_kv = PE.mark()
                tok_kv_prev = t_kv
                SC.wait(PE, t_kv)
                SC.op(nc.scalar.copy(kfm[:, 512 * g4:512 * (g4 + 1)], ps_k[:]))
                t_ke = SC.mark()
                SC.h.wait_ge(SC.sem, t_ke)
                if tok_vfm_user:
                    SC.wait(PE, tok_vfm_user)
                SC.op(nc.scalar.copy(vfm_bf[:], ps_v[:]))
                t_vev = SC.mark()
                PE.wait(SC, t_vev)
                for tt in range(4):
                    rr = state["pst_rr"][0] % 4
                    state["pst_rr"][0] += 1
                    pst = psT[rr]
                    if state["pst_evac"][rr]:
                        e_, t_ = state["pst_evac"][rr]
                        PE.wait(e_, t_)
                    PE.op(nc.tensor.transpose(pst[:, 0, :],
                                              vfm_bf[:, 128 * tt:128 * (tt + 1)], ident[:]))
                    t_tv = PE.mark()
                    SC.wait(PE, t_tv)
                    SC.op(nc.scalar.copy(vtiles[:, 4 * g4 + tt, 0:128], pst[:, 0, :]))
                    state["pst_evac"][rr] = (SC, SC.mark())
                tok_vfm_user = PE.n

            for t in range(NT if P1A_OWN else 0):
                par = t % 2
                for (ue, ut) in state["xt_user"][par]:
                    SP.wait(ue, ut)
                tk = dXQ.start(t, xt[:, par, :], xq[128 * t:128 * (t + 1), :])

                def lnT_dst(fg, t=t):
                    return lnT_own[:, 8 * fg:8 * fg + 8, 128 * t:128 * (t + 1)]

                _, t_ap, t_ag = ln_tile_pipeline(xt[:, par, :], par, dXQ, tk, lnT_dst)
                state["xt_user"][par] = [(SC, t_ap), (VE, t_ag)]

        tok_lnT_own_ready = SC.mark()
        tok_lnT_own_ready_ve = VE.mark()

        if DEBUG and BUILD_PHASES >= 1:
            barrier()
            SP.h.wait_ge(SC.sem, tok_lnT_own_ready)
            gPC.start(3, taps["t_kfm"][:], kfm[:])
            gPC.start(3, taps["t_v"][:], vtiles[:])
            gPC.start(3, taps["t_lnT"][:], lnT_own[:])
            tap_quiesce()

        if BUILD_PHASES >= 2:
            barrier()
            p1a.close()
            attn_out = pa.enter_context(
                nc.sbuf_tensor("attn_out", [128, NT, D], BF16, side="right"))
            dmask_s = sb1("dmask_s", [128, WD], F16)
            gW.qh.wait_ge(gW.sems[0], gW.cnt[0])
            tk_dm = gW.start(0, dmask_s[:], dmask[:])
            q_bf = sb1("q_bf", [128, 2, T], BF16)
            probs = sb1("probs", [128, 3, 512])
            probsT = sb1("probsT", [128, 3, 512], BF16)
            wq_bf = sb1("wq_bf", [128, 2, NF, 128], BF16)
            rec = sb1("rec", [128, 8])
            ps_q = p1.enter_context(nc.psum_tensor("ps_q", [128, 512], F32))
            ps_s = p1.enter_context(nc.psum_tensor("ps_s", [128, 3, 512], F32))
            pvt = [p1.enter_context(nc.psum_tensor(f"pv{k}", [128, 129], F32))
                   for k in range(4)]
            dmv2 = dmask_s[:].rearrange("p (a b) -> p a b", b=128)

            tok_wq = [None, None]
            tok_qevac_last = 0
            tok_qbf_user = [0, 0]
            tok_ps_s_user = [0, 0, 0]
            tok_probs_user = [0, 0, 0]
            tok_probsT_user = [0, 0, 0]
            tok_pv_user = [0] * 4
            tok_rec_user = [0]
            tok_qgen_done = 0

            sWQ.qh.wait_ge(gPC.sems[tk_pc_q[0]], tk_pc_q[1])
            tok_wq[0] = sWQ.start(0, wq_bf[:, 0, :, :],
                                  wq_bf_d[:, 0:128].rearrange("(c p) n -> p c n", p=128))
            gW.wait(VE, tk_dm)

            sjob = 0

            def emit_qgen(hh, par_, out_tok):
                acts = []
                st_ = {"first": True, "ev0": 0}
                qs = 1.0 / (SLOPES[hh] * math.sqrt(HD))

                def mk(f, half):
                    def go():
                        if st_["first"]:
                            sWQ.wait(PE, tok_wq[par_])
                            PE.wait(SC, tok_lnT_own_ready)
                            PE.wait(VE, tok_lnT_own_ready_ve)
                            if tok_qevac_last[0]:
                                PE.wait(SC, tok_qevac_last[0])
                            st_["first"] = False
                        if half == 1 and f == 0:
                            PE.wait(SC, st_["ev0"])
                        PE.op(nc.tensor.matmul(ps_q[:], wq_bf[:, par_, f, :],
                                               lnT_own[:, f, 512 * half:512 * (half + 1)],
                                               start=(f == 0), stop=(f == NF - 1)))
                        if f == NF - 1:
                            t_g = PE.mark()
                            out_tok["qgen_done"] = t_g
                            SC.wait(PE, t_g)
                            if half == 0 and tok_qbf_user[par_]:
                                SC.wait(PE, tok_qbf_user[par_])
                            SC.op(nc.scalar.activation(
                                q_bf[:, par_, 512 * half:512 * (half + 1)],
                                ps_q[:], AF.Identity, scale=qs))
                            tk = SC.mark()
                            if half == 0:
                                st_["ev0"] = tk
                            else:
                                out_tok["q_ready"] = tk
                                tok_qevac_last[0] = tk
                    return go
                for half in range(2):
                    for f in range(NF):
                        acts.append(mk(f, half))
                return acts

            tok_qevac_last = [0]

            h0_tok = {}
            for th in emit_qgen(0, 0, h0_tok):
                th()
            tok_qgen_done = h0_tok["qgen_done"]
            tok_q_cur = h0_tok["q_ready"]

            pending_wq = [1]

            def issue_wq():
                hh = pending_wq[0]
                if hh < NH:
                    sWQ.qh.wait_ge(PE.sem, tok_qgen_done)
                    tok_wq[hh % 2] = sWQ.start(hh % 2, wq_bf[:, hh % 2, :, :],
                                               wq_bf_d[:, 128 * hh:128 * (hh + 1)]
                                               .rearrange("(c p) n -> p c n", p=128))
                    pending_wq[0] = hh + 1

            issue_wq()
            for h in range(NH):
                par = h % 2
                nxt = (h + 1) % 2
                nx_tok = {}
                filler = emit_qgen(h + 1, nxt, nx_tok) if h + 1 < NH else []
                fill_i = 0

                nw = NW[h]
                slope = SLOPES[h]
                jobs = []
                for grp in range(2):
                    glo, ghi = 4 * grp, 4 * grp + 3
                    ranges = []
                    for kc in range(NKC):
                        i_lo, i_hi = _i_range(kc, nw)
                        i0, i1 = max(i_lo, glo), min(i_hi, ghi)
                        if i0 <= i1:
                            ranges.append((kc, i0, i1))
                    k = 0
                    while k < len(ranges):
                        kcs = [ranges[k][0]]
                        ri0, ri1 = ranges[k][1], ranges[k][2]
                        k2 = k + 1
                        while k2 < len(ranges) and ranges[k2][0] == kcs[-1] + 1:
                            n_i0 = min(ri0, ranges[k2][1])
                            n_i1 = max(ri1, ranges[k2][2])
                            if (len(kcs) + 1) * 128 * (n_i1 - n_i0 + 1) > 512:
                                break
                            kcs.append(ranges[k2][0])
                            ri0, ri1 = n_i0, n_i1
                            k2 += 1
                        jobs.append((list(reversed(kcs)), ri0, ri1, glo))
                        k = k2

                def emit_scores_dve_act(job, sj):
                    kcs, i0, i1, glo = job
                    nch = i1 - i0 + 1
                    rcols = 128 * nch
                    nkc = len(kcs)
                    cols = rcols * nkc
                    slot = sj % 3
                    pp = sj % 3
                    if tok_ps_s_user[slot]:
                        PE.wait(VE, tok_ps_s_user[slot])
                    for kx, kc in enumerate(kcs):
                        PE.op(nc.tensor.matmul(
                            ps_s[:, slot, rcols * kx:rcols * (kx + 1)],
                            kfm[:, 128 * kc:128 * (kc + 1)],
                            q_bf[:, par, 128 * i0:128 * i0 + rcols],
                            start=True, stop=True))
                    t_sc = PE.mark()
                    VE.wait(PE, t_sc)
                    if tok_probs_user[pp]:
                        VE.wait(SC, tok_probs_user[pp])
                    base_hi = C0 + 128 * (4 * i0 - kcs[0])
                    s0 = base_hi // 128
                    dview = dmv2[:, s0:s0 + 4 * nch] \
                        .rearrange("p (c k) b -> p c k b", k=4) \
                        .rearrange("p c k b -> p k c b")[:, 0:nkc, :, :]
                    pview = probs[:, pp, 0:cols].rearrange(
                        "p (a b c) -> p a b c", a=nkc, c=128)
                    sview = ps_s[:, slot, 0:cols].rearrange(
                        "p (a b c) -> p a b c", a=nkc, c=128)
                    VE.op(nc.vector.tensor_tensor(pview, sview, dview, ALU.add))
                    t_da = VE.mark()
                    tok_ps_s_user[slot] = t_da
                    SC.wait(VE, t_da)
                    if tok_probsT_user[pp]:
                        SC.wait(PE, tok_probsT_user[pp])
                    SC.op(nc.scalar.activation(probsT[:, pp, 0:cols],
                                               probs[:, pp, 0:cols], AF.Exp,
                                               scale=slope))
                    t_ex = SC.mark()
                    tok_probs_user[pp] = t_ex
                    return t_ex

                def emit_pv(job, sj, t_ex):
                    kcs, i0, i1, glo = job
                    nch = i1 - i0 + 1
                    rcols = 128 * nch
                    pp = sj % 3
                    PE.wait(SC, t_ex)
                    stopped = []
                    for kx in range(len(kcs) - 1, -1, -1):
                        kc = kcs[kx]
                        gi0, gi1 = _i_range(kc, nw)
                        for ii in range(nch):
                            i = i0 + ii
                            if i < max(gi0, glo) or i > min(gi1, glo + 3):
                                continue
                            st = (kc == _start_kc(i, nw))
                            spf = (kc == 4 * i + 3)
                            if spf:
                                stopped.append(i)
                            if st and tok_pv_user[i - glo]:
                                PE.wait(SC, tok_pv_user[i - glo])
                            PE.op(nc.tensor.matmul(
                                pvt[i - glo][:],
                                probsT[:, pp, rcols * kx + 128 * ii:
                                       rcols * kx + 128 * (ii + 1)],
                                vtiles[:, kc, :], start=st, stop=spf))
                    t_pvt = PE.mark()
                    tok_probsT_user[pp] = t_pvt
                    for i in stopped:
                        if True:
                            il = i - glo
                            VE.wait(PE, t_pvt)
                            if tok_rec_user[0]:
                                VE.wait(SC, tok_rec_user[0])
                            VE.op(nc.vector.reciprocal(rec[:, i:i + 1],
                                                       pvt[il][:][:, 128:129]))
                            t_rc = VE.mark()
                            SC.wait(VE, t_rc)
                            SC.op(nc.scalar.activation(
                                attn_out[:, i, 128 * h:128 * (h + 1)],
                                pvt[il][:][:, 0:128], AF.Identity,
                                scale=rec[:, i:i + 1]))
                            tok_pv_user[il] = SC.mark()
                            tok_rec_user[0] = tok_pv_user[il]

                pend = []
                first_mm = True
                for job in jobs:
                    if first_mm:
                        PE.wait(SC, tok_q_cur)
                        first_mm = False
                    t_ex = emit_scores_dve_act(job, sjob)
                    if len(pend) >= 2:
                        pj = pend.pop(0)
                        emit_pv(pj[0], pj[1], pj[2])
                    for _ in range(3):
                        if fill_i < len(filler):
                            filler[fill_i]()
                            fill_i += 1
                    if fill_i >= len(filler) and pending_wq[0] == h + 2:
                        tok_qgen_done = nx_tok["qgen_done"]
                        issue_wq()
                    pend.append((job, sjob, t_ex))
                    sjob += 1
                for pj in pend:
                    emit_pv(pj[0], pj[1], pj[2])
                while fill_i < len(filler):
                    filler[fill_i]()
                    fill_i += 1
                tok_qbf_user[par] = PE.n
                if h + 1 < NH:
                    tok_qgen_done = nx_tok["qgen_done"]
                    if pending_wq[0] == h + 2:
                        issue_wq()
                    tok_q_cur = nx_tok["q_ready"]

        if DEBUG and BUILD_PHASES >= 2:
            barrier()
            SP.h.wait_ge(SC.sem, SC.n)
            gPC.start(3, taps["t_attn"][:], attn_out[:])
            tap_quiesce()

        x_res = None
        if BUILD_PHASES >= 3:
            barrier()
            p1.close()
            p1c = ExitStack()
            sbc = lambda nm, shp, dt=F32: p1c.enter_context(nc.sbuf_tensor(nm, shp, dt))
            attn_outT = sbc("attn_outT", [128, NF, T], BF16)
            pt = ExitStack()
            psTc = [pt.enter_context(nc.psum_tensor(f"ps_tc{k}", [128, 8, 128], BF16))
                    for k in range(4)]
            tok_ev = [(None, 0)] * 4
            rrc = 0
            for t in range(NT):
                for fg in range(4):
                    rr = rrc % 4
                    rrc += 1
                    pst = psTc[rr]
                    ev_eng = SC if (fg % 2 == 0) else VE
                    e_, t_ = tok_ev[rr]
                    if e_ is not None:
                        PE.wait(e_, t_)
                    for ff in range(8):
                        f = 8 * fg + ff
                        PE.op(nc.tensor.transpose(pst[:, ff, :],
                                                  attn_out[:, t, 128 * f:128 * (f + 1)],
                                                  ident[:]))
                    t_tr = PE.mark()
                    ev_eng.wait(PE, t_tr)
                    if ev_eng is SC:
                        ev_eng.op(nc.scalar.copy(
                            attn_outT[:, 8 * fg:8 * fg + 8, 128 * t:128 * (t + 1)],
                            pst[:, :, :]))
                    else:
                        ev_eng.op(nc.vector.tensor_copy(
                            attn_outT[:, 8 * fg:8 * fg + 8, 128 * t:128 * (t + 1)],
                            pst[:, :, :]))
                    tok_ev[rr] = (ev_eng, ev_eng.mark())
            barrier()
            pa.close()
            pt.close()
            x_res = top.enter_context(
                nc.sbuf_tensor("x_res", [128, NT, D], F32, side="right"))

            ps_wo = p1c.enter_context(nc.psum_tensor("ps_wo", [128, NT, 512], F32))
            wo_bf = sbc("wo_bf", [128, 2, 2, 512], BF16)
            xs_buf = sbc("xs_buf", [128, 2, 512])
            tok_wo = [None, None]
            tok_wo_user = [0, 0]
            tok_xs_user = [0, 0]
            tok_pswo_user = [0] * NT
            for os_ in range(8):
                for fg in range(16):
                    sl = (os_ * 16 + fg) % 2
                    if tok_wo_user[sl]:
                        gWO.qh.wait_ge(PE.sem, tok_wo_user[sl])
                    if os_ == 0 and fg == 0:
                        gWO.qh.wait_ge(gPC.sems[tk_pc_wo[0]], tk_pc_wo[1])
                    tok_wo[sl] = gWO.start(sl, wo_bf[:, sl, :, :],
                                           wo_bf_d[256 * fg:256 * (fg + 1),
                                                   512 * os_:512 * (os_ + 1)]
                                           .rearrange("(a p) n -> p a n", p=128))
                    gWO.wait(PE, tok_wo[sl])
                    for ff in range(2):
                        f = 2 * fg + ff
                        for t in range(NT):
                            st = (f == 0)
                            spf = (f == NF - 1)
                            if st and tok_pswo_user[t]:
                                PE.wait(VE, tok_pswo_user[t])
                            PE.op(nc.tensor.matmul(ps_wo[:, t, :],
                                                   attn_outT[:, f, 128 * t:128 * (t + 1)],
                                                   wo_bf[:, sl, ff, :],
                                                   start=st, stop=spf))
                    tok_wo_user[sl] = PE.mark()
                t_wod = PE.n
                for t in range(NT):
                    spar = t % 2
                    if tok_xs_user[spar]:
                        SP.wait(VE, tok_xs_user[spar])
                    tkx = dXQ.start(spar, xs_buf[:, spar, :],
                                    xq[128 * t:128 * (t + 1), 512 * os_:512 * (os_ + 1)])
                    dXQ.wait(VE, tkx)
                    VE.wait(PE, t_wod)
                    VE.op(nc.vector.tensor_tensor(
                        x_res[:, t, 512 * os_:512 * (os_ + 1)],
                        ps_wo[:, t, :], xs_buf[:, spar, :], ALU.add))
                    t_x1 = VE.mark()
                    tok_pswo_user[t] = t_x1
                    tok_xs_user[spar] = t_x1
            barrier()
            p1c.close()
            if DEBUG:
                SP.h.wait_ge(VE.sem, VE.n)
                dST.start(0, taps["t_x1"][:], x_res[:])
                tap_quiesce()

        def emit_ffn(store_out):
            tag = "b" if store_out else "a"
            pf = ExitStack()
            sbf = lambda nm, shp, dt=F32: pf.enter_context(
                nc.sbuf_tensor(nm + tag, shp, dt))
            xa = sbf("xa", [128, D])
            xa_bf = sbf("xa_bf", [128, D], BF16)
            xaT = sbf("xaT", [128, NF, 128], BF16)
            Lbuf = sbf("Lbuf", [128, NT, 24])
            stat_m = sbf("stat_m", [128, NT])
            stat_v = sbf("stat_v", [128, NT])
            stat_r = sbf("stat_r", [128, NT])
            bn6f = sbf("bn6f", [128, 8, 6])
            bnsf = sbf("bnsf", [128, 2])
            mbf = sbf("mbf", [128, 1])
            rmsc = sbf("rmsc", [128, 1])
            Z = sbf("Z", [128, NT, 4, 4])
            ub = sbf("ub", [128, NT, 4])
            vb = sbf("vb", [128, NT, 4])
            Tb = sbf("Tb", [128, NT, 4, 4])
            Kp = sbf("Kp", [128, NT, 4, 4])
            Mb = sbf("Mb", [128, NT, 4])
            Sb = sbf("Sb", [128, NT, 4])
            Lb = sbf("Lb", [128, NT, 4])
            smax = sbf("smax", [128, NT])
            hpre = sbf("hpre", [128, NT, 4])
            hpost = sbf("hpost", [128, NT, 4])
            hres = sbf("hres", [128, NT, 16])
            xp = sbf("xp", [128, 1024])
            xp_bf = sbf("xp_bf", [128, 1024], BF16)
            xpT = sbf("xpT", [128, 8, 128], BF16)
            y_f32 = sbf("y_f32", [128, 1024])
            wbr_s = sbf("wbr_s", [128, 8, 1024], BF16)
            gW.qh.wait_ge(gW.sems[1], gW.cnt[1])
            tk_wbr = gW.start(1, wbr_s[:], wbr[:].rearrange("(c p) n -> p c n", p=128))
            ps_L = pf.enter_context(nc.psum_tensor("ps_L" + tag, [128, 24], F32))
            ps_tf0 = pf.enter_context(nc.psum_tensor("ps_tf0" + tag, [128, 8, 128], BF16))
            ps_tf1 = pf.enter_context(nc.psum_tensor("ps_tf1" + tag, [128, 8, 128], BF16))
            ps_y = pf.enter_context(nc.psum_tensor("ps_y" + tag, [128, 1024], F32))

            tA_ev = [0, 0]
            tok_bn_user_ve = 0
            tok_bn_user_sc = 0
            tok_xabf_user = 0
            tok_psL_user = 0
            tok_rms_user = 0
            for t in range(NT):
                if tok_bn_user_ve:
                    VE.h.wait_ge(VE.sem, tok_bn_user_ve)
                if tok_bn_user_sc:
                    VE.wait(SC, tok_bn_user_sc)
                for g8 in range(8):
                    VE.op(nc.vector.bn_stats(bn6f[:, g8, :],
                                             x_res[:, t, 512 * g8:512 * (g8 + 1)]))
                t_bn = VE.mark()
                VE.h.wait_ge(VE.sem, t_bn)
                VE.op(nc.vector.bn_aggr(bnsf[:], bn6f[:]))
                t_ag = VE.mark()
                VE.h.wait_ge(VE.sem, t_ag)
                VE.op(nc.vector.tensor_copy(stat_m[:, t:t + 1], bnsf[:, 0:1]))
                VE.op(nc.vector.tensor_copy(stat_v[:, t:t + 1], bnsf[:, 1:2]))
                t_cp = VE.mark()
                tok_bn_user_ve = t_cp
                SC.wait(VE, t_cp)
                SC.op(nc.scalar.activation(stat_r[:, t:t + 1], stat_v[:, t:t + 1],
                                           AF.Ln, bias=eps1[:]))
                t_l1 = SC.mark()
                SC.h.wait_ge(SC.sem, t_l1)
                SC.op(nc.scalar.activation(stat_r[:, t:t + 1], stat_r[:, t:t + 1],
                                           AF.Exp, scale=-0.5))
                t_r = SC.mark()
                tok_bn_user_sc = t_r
                VE.wait(SC, t_r)
                if tok_rms_user:
                    VE.wait(SC, tok_rms_user)
                VE.op(nc.vector.tensor_scalar(rmsc[:], stat_v[:, t:t + 1],
                                              stat_r[:, t:t + 1], stat_r[:, t:t + 1],
                                              ALU.mult, ALU.mult))
                t_r2 = VE.mark()
                SC.wait(VE, t_r2)
                SC.op(nc.scalar.activation(rmsc[:], rmsc[:], AF.Ln, bias=epsr[:]))
                t_l2 = SC.mark()
                SC.h.wait_ge(SC.sem, t_l2)
                SC.op(nc.scalar.activation(rmsc[:], rmsc[:], AF.Exp, scale=-0.5))
                t_rms = SC.mark()
                VE.wait(SC, t_r)
                VE.op(nc.vector.tensor_scalar(mbf[:], stat_m[:, t:t + 1],
                                              stat_r[:, t:t + 1], -1.0,
                                              ALU.mult, ALU.mult))
                t_mb = VE.mark()
                SC.wait(VE, t_mb)
                if tok_xabf_user:
                    SC.wait(PE, tok_xabf_user)
                SC.op(nc.scalar.activation(xa_bf[:], x_res[:, t, :], AF.Identity,
                                           bias=mbf[:], scale=stat_r[:, t:t + 1]))
                t_xa = SC.mark()
                PE.wait(SC, t_xa)
                for fg in range(4):
                    pp = fg % 2
                    pst = ps_tf0 if pp == 0 else ps_tf1
                    if tA_ev[pp]:
                        PE.wait(SC, tA_ev[pp])
                    for ff in range(8):
                        f = 8 * fg + ff
                        PE.op(nc.tensor.transpose(pst[:, ff, :],
                                                  xa_bf[:, 128 * f:128 * (f + 1)],
                                                  ident[:]))
                    t_tr = PE.mark()
                    SC.wait(PE, t_tr)
                    SC.op(nc.scalar.copy(xaT[:, 8 * fg:8 * fg + 8, :], pst[:, :, :]))
                    tA_ev[pp] = SC.mark()
                tok_xabf_user = PE.n
                PE.wait(SC, tA_ev[0])
                PE.wait(SC, tA_ev[1])
                gW.wait(PE, tk_w0)
                if tok_psL_user:
                    PE.wait(SC, tok_psL_user)
                for f in range(NF):
                    PE.op(nc.tensor.matmul(ps_L[:], xaT[:, f, :], wrpp_bf[:, f, :],
                                           start=(f == 0), stop=(f == NF - 1)))
                t_lg = PE.mark()
                SC.wait(PE, t_lg)
                SC.h.wait_ge(SC.sem, t_rms)
                SC.op(nc.scalar.activation(Lbuf[:, t, :], ps_L[:], AF.Identity,
                                           scale=rmsc[:]))
                t_le = SC.mark()
                tok_psL_user = t_le
                tok_rms_user = t_le
                dCN.wait(VE, tk_cn)
                VE.wait(SC, t_le)
                VE.op(nc.vector.tensor_tensor(Lbuf[:, t, :], Lbuf[:, t, :], bspp[:],
                                              ALU.add))
                VE.mark()

            t_A = VE.n
            VE.h.wait_ge(VE.sem, t_A)
            last_exp_sc = 0
            for (lo, dst) in ((16, hpre), (20, hpost)):
                VE.h.wait_ge(VE.sem, VE.n)
                Lv = Lbuf[:, :, lo:lo + 4]
                VE.op(nc.vector.reduce_max(
                    smax[:].rearrange("p (a o) -> p a o", o=1), Lv, AX.X))
                t_m = VE.mark()
                VE.h.wait_ge(VE.sem, t_m)
                smb = smax[:].rearrange("p (a o) -> p a o", o=1).broadcast_to([128, NT, 4])
                VE.op(nc.vector.tensor_sub(dst[:], Lv, smb))
                t_s = VE.mark()
                SC.wait(VE, t_s)
                SC.op(nc.scalar.activation(dst[:], dst[:], AF.Exp))
                t_e = SC.mark()
                last_exp_sc = t_e
                VE.wait(SC, t_e)
                VE.op(nc.vector.reduce_sum(
                    smax[:].rearrange("p (a o) -> p a o", o=1), dst[:], AX.X))
                t_ss = VE.mark()
                VE.h.wait_ge(VE.sem, t_ss)
                VE.op(nc.vector.reciprocal(smax[:], smax[:]))
                t_rcp = VE.mark()
                VE.h.wait_ge(VE.sem, t_rcp)
                VE.op(nc.vector.tensor_mul(dst[:], dst[:], smb))
                VE.mark()
            VE.op(nc.vector.tensor_scalar_mul(
                Z[:].rearrange("p a i j -> p a (i j)"), Lbuf[:, :, 0:16], TAU_INV))
            t_z = VE.mark()
            VE.h.wait_ge(VE.sem, t_z)
            PL.op(nc.gpsimd.memset(ub[:], 0.0))
            PL.op(nc.gpsimd.memset(vb[:], 0.0))
            t_z0 = PL.mark()
            VE.wait(PL, t_z0)

            st_sink = {"t4": last_exp_sc}

            def half_iter(z_view, pot_in, pot_out):
                if st_sink["t4"]:
                    VE.wait(SC, st_sink["t4"])
                pib = pot_in[:].rearrange("p a (o i) -> p a o i", o=1) \
                    .broadcast_to([128, NT, 4, 4])
                VE.op(nc.vector.tensor_sub(Tb[:], z_view, pib))
                t1 = VE.mark()
                VE.h.wait_ge(VE.sem, t1)
                VE.op(nc.vector.reduce_max(Mb[:], Tb[:], AX.X))
                t2 = VE.mark()
                VE.h.wait_ge(VE.sem, t2)
                mb2 = Mb[:].rearrange("p a (i o) -> p a i o", o=1) \
                    .broadcast_to([128, NT, 4, 4])
                VE.op(nc.vector.tensor_sub(Tb[:], Tb[:], mb2))
                t3 = VE.mark()
                SC.wait(VE, t3)
                SC.op(nc.scalar.activation(Tb[:], Tb[:], AF.Exp))
                t4 = SC.mark()
                st_sink["t4"] = t4
                VE.wait(SC, t4)
                VE.op(nc.vector.reduce_sum(Sb[:], Tb[:], AX.X))
                t5 = VE.mark()
                SC.wait(VE, t5)
                SC.op(nc.scalar.activation(Lb[:], Sb[:], AF.Ln))
                t6 = SC.mark()
                VE.wait(SC, t6)
                VE.op(nc.vector.tensor_add(pot_out[:], Lb[:], Mb[:]))
                t7 = VE.mark()
                VE.h.wait_ge(VE.sem, t7)

            zv = Z[:]
            zvT = Z[:].rearrange("p a i j -> p a j i")
            half_iter(zv, vb, ub)
            half_iter(zvT, ub, vb)
            VE.wait(SC, st_sink["t4"])
            uib = ub[:].rearrange("p a (i o) -> p a i o", o=1).broadcast_to([128, NT, 4, 4])
            vjb = vb[:].rearrange("p a (o i) -> p a o i", o=1).broadcast_to([128, NT, 4, 4])
            VE.op(nc.vector.tensor_sub(Tb[:], Z[:], uib))
            t_k1 = VE.mark(); VE.h.wait_ge(VE.sem, t_k1)
            VE.op(nc.vector.tensor_sub(Tb[:], Tb[:], vjb))
            t_k2 = VE.mark()
            SC.wait(VE, t_k2)
            SC.op(nc.scalar.activation(Kp[:], Tb[:], AF.Exp))
            t_kp = SC.mark()
            VE.wait(SC, t_kp)
            VE.op(nc.vector.memset(ub[:], 1.0))
            VE.op(nc.vector.memset(vb[:], 1.0))
            t_ab0 = VE.mark(); VE.h.wait_ge(VE.sem, t_ab0)
            KpT = Kp[:].rearrange("p a i j -> p a j i")
            for _ in range(SINK_ITERS - 1):
                bb = vb[:].rearrange("p a (o i) -> p a o i", o=1) \
                    .broadcast_to([128, NT, 4, 4])
                VE.op(nc.vector.tensor_mul(Tb[:], Kp[:], bb))
                t1 = VE.mark(); VE.h.wait_ge(VE.sem, t1)
                VE.op(nc.vector.reduce_sum(Sb[:], Tb[:], AX.X))
                t2 = VE.mark(); VE.h.wait_ge(VE.sem, t2)
                VE.op(nc.vector.reciprocal(ub[:], Sb[:]))
                t3 = VE.mark(); VE.h.wait_ge(VE.sem, t3)
                ab = ub[:].rearrange("p a (o i) -> p a o i", o=1) \
                    .broadcast_to([128, NT, 4, 4])
                VE.op(nc.vector.tensor_mul(Tb[:], KpT, ab))
                t4 = VE.mark(); VE.h.wait_ge(VE.sem, t4)
                VE.op(nc.vector.reduce_sum(Sb[:], Tb[:], AX.X))
                t5 = VE.mark(); VE.h.wait_ge(VE.sem, t5)
                VE.op(nc.vector.reciprocal(vb[:], Sb[:]))
                t6 = VE.mark(); VE.h.wait_ge(VE.sem, t6)
            bb = vb[:].rearrange("p a (o i) -> p a o i", o=1).broadcast_to([128, NT, 4, 4])
            VE.op(nc.vector.tensor_mul(Tb[:], Kp[:], bb))
            t_h1 = VE.mark(); VE.h.wait_ge(VE.sem, t_h1)
            aib = ub[:].rearrange("p a (i o) -> p a i o", o=1).broadcast_to([128, NT, 4, 4])
            VE.op(nc.vector.tensor_mul(
                hres[:].rearrange("p a (i j) -> p a i j", j=4), Tb[:], aib))
            t_h4 = VE.mark(); VE.h.wait_ge(VE.sem, t_h4)
            VE.wait(VE, t_g3)
            VE.op(nc.vector.tensor_scalar_mul(hres[:], hres[:], gcol[:]))
            t_h5 = VE.mark(); VE.h.wait_ge(VE.sem, t_h5)
            for dd in range(4):
                VE.op(nc.vector.tensor_scalar_add(hres[:, :, 5 * dd:5 * dd + 1],
                                                  hres[:, :, 5 * dd:5 * dd + 1],
                                                  g1col[:]))
            t_hres = VE.mark()

            gW.wait(PE, tk_wbr)
            tok_xa_user = 0
            tok_y_user = 0
            tok_psy_sc = 0
            tok_xpbf_user = 0
            tok_xpT_user = 0
            st_toks = []
            for t in range(NT):
                VE.h.wait_ge(VE.sem, t_hres)
                if tok_xa_user:
                    VE.h.wait_ge(VE.sem, tok_xa_user)
                    VE.wait(SC, tok_xpbf_user)
                VE.op(nc.vector.tensor_scalar(xa[:], x_res[:, t, :], stat_m[:, t:t + 1],
                                              stat_r[:, t:t + 1], ALU.subtract, ALU.mult))
                t_xa = VE.mark()
                VE.h.wait_ge(VE.sem, t_xa)
                VE.op(nc.vector.tensor_scalar_mul(xp[:], xa[:, 0:1024], hpre[:, t, 0:1]))
                VE.mark()
                VE.h.wait_ge(VE.sem, VE.n)
                for n in range(1, 4):
                    VE.op(nc.vector.scalar_tensor_tensor(
                        xp[:], xa[:, 1024 * n:1024 * (n + 1)], hpre[:, t, n:n + 1],
                        xp[:], ALU.mult, ALU.add))
                    VE.mark()
                    VE.h.wait_ge(VE.sem, VE.n)
                SC.wait(VE, VE.n)
                if tok_xpT_user:
                    SC.wait(PE, tok_xpT_user)
                SC.op(nc.scalar.copy(xp_bf[:], xp[:]))
                t_xpb = SC.mark()
                tok_xpbf_user = t_xpb
                PE.wait(SC, t_xpb)
                pst = ps_tf0 if (t % 2 == 0) else ps_tf1
                if tA_ev[t % 2]:
                    PE.wait(SC, tA_ev[t % 2])
                for c in range(8):
                    PE.op(nc.tensor.transpose(pst[:, c, :],
                                              xp_bf[:, 128 * c:128 * (c + 1)],
                                              ident[:]))
                t_tr = PE.mark()
                SC.wait(PE, t_tr)
                SC.op(nc.scalar.copy(xpT[:], pst[:, :, :]))
                tA_ev[t % 2] = SC.mark()
                tok_xpT_user = PE.n
                PE.wait(SC, tA_ev[0])
                PE.wait(SC, tA_ev[1])
                if tok_psy_sc:
                    PE.wait(SC, tok_psy_sc)
                for c in range(8):
                    PE.op(nc.tensor.matmul(ps_y[:, 0:512], xpT[:, c, :],
                                           wbr_s[:, c, 0:512],
                                           start=(c == 0), stop=(c == 7)))
                    PE.op(nc.tensor.matmul(ps_y[:, 512:1024], xpT[:, c, :],
                                           wbr_s[:, c, 512:1024],
                                           start=(c == 0), stop=(c == 7)))
                t_ym = PE.mark()
                tok_xpT_user = t_ym
                SC.wait(PE, t_ym)
                if tok_y_user:
                    SC.wait(VE, tok_y_user)
                SC.op(nc.scalar.copy(y_f32[:], ps_y[:]))
                t_ye = SC.mark()
                tok_psy_sc = t_ye
                VE.wait(SC, t_ye)
                chain_tok = [0, 0, 0, 0]
                for jn in range(5):
                    for i in range(4):
                        dst = x_res[:, t, 1024 * i:1024 * (i + 1)]
                        if chain_tok[i]:
                            VE.h.wait_ge(VE.sem, chain_tok[i])
                        if jn < 4:
                            VE.op(nc.vector.scalar_tensor_tensor(
                                dst, xa[:, 1024 * jn:1024 * (jn + 1)],
                                hres[:, t, 4 * i + jn:4 * i + jn + 1], dst,
                                ALU.mult, ALU.add))
                        else:
                            VE.op(nc.vector.scalar_tensor_tensor(
                                dst, y_f32[:], hpost[:, t, i:i + 1], dst,
                                ALU.mult, ALU.add))
                        chain_tok[i] = VE.mark()
                tok_y_user = VE.n
                tok_xa_user = VE.n
                if store_out:
                    SP.wait(VE, VE.n)
                    st_toks.append(dST.start(t, y[128 * t:128 * (t + 1), :],
                                             x_res[:, t, :]))
            pf.close()
            return st_toks

        if BUILD_PHASES >= 4:
            emit_ffn(store_out=False)
            barrier()
            if DEBUG:
                SP.h.wait_ge(VE.sem, VE.n)
                dST.start(1, taps["t_x2"][:], x_res[:])
                tap_quiesce()

        if BUILD_PHASES >= 5:
            p3 = ExitStack()
            sb3 = lambda nm, shp, dt=F32: p3.enter_context(nc.sbuf_tensor(nm, shp, dt))
            ln2T = sb3("ln2T", [128, NF, T], BF16)
            pb = ExitStack()
            lnb = pb.enter_context(nc.sbuf_tensor("lnb3", [128, D], BF16))
            bn63 = pb.enter_context(nc.sbuf_tensor("bn63", [128, 8, 6], F32))
            bns3 = pb.enter_context(nc.sbuf_tensor("bns3", [128, 2], F32))
            r3 = pb.enter_context(nc.sbuf_tensor("r3", [128, 1], F32))
            mb3 = pb.enter_context(nc.sbuf_tensor("mb3", [128, 1], F32))
            psT3 = [pb.enter_context(nc.psum_tensor(f"ps_t3{k}", [128, 8, 128], BF16))
                    for k in range(4)]
            tok_ev3 = [(None, 0)] * 4
            tok_bn3_ve = 0
            tok_bn3_sc = 0
            tok_lnb_user = 0
            for t in range(NT):
                if tok_bn3_ve:
                    VE.h.wait_ge(VE.sem, tok_bn3_ve)
                if tok_bn3_sc:
                    VE.wait(SC, tok_bn3_sc)
                for g8 in range(8):
                    VE.op(nc.vector.bn_stats(bn63[:, g8, :],
                                             x_res[:, t, 512 * g8:512 * (g8 + 1)]))
                t_bn = VE.mark()
                VE.h.wait_ge(VE.sem, t_bn)
                VE.op(nc.vector.bn_aggr(bns3[:], bn63[:]))
                t_ag = VE.mark()
                SC.wait(VE, t_ag)
                SC.op(nc.scalar.activation(r3[:], bns3[:, 1:2], AF.Ln, bias=eps1[:]))
                t_l = SC.mark()
                SC.h.wait_ge(SC.sem, t_l)
                SC.op(nc.scalar.activation(r3[:], r3[:], AF.Exp, scale=-0.5))
                t_r = SC.mark()
                VE.wait(SC, t_r)
                VE.op(nc.vector.tensor_scalar(mb3[:], bns3[:, 0:1], r3[:], -1.0,
                                              ALU.mult, ALU.mult))
                t_mb = VE.mark()
                tok_bn3_ve = t_mb
                SC.wait(VE, t_mb)
                if tok_lnb_user:
                    SC.wait(PE, tok_lnb_user)
                SC.op(nc.scalar.activation(lnb[:], x_res[:, t, :], AF.Identity,
                                           bias=mb3[:], scale=r3[:]))
                t_lnb = SC.mark()
                tok_bn3_sc = t_lnb
                PE.wait(SC, t_lnb)
                for fg in range(4):
                    rr = (t * 4 + fg) % 4
                    pst = psT3[rr]
                    ev_eng = SC if (fg % 2 == 0) else VE
                    e_, t_ = tok_ev3[rr]
                    if e_ is not None:
                        PE.wait(e_, t_)
                    for ff in range(8):
                        f = 8 * fg + ff
                        PE.op(nc.tensor.transpose(pst[:, ff, :],
                                                  lnb[:, 128 * f:128 * (f + 1)],
                                                  ident[:]))
                    t_tr = PE.mark()
                    ev_eng.wait(PE, t_tr)
                    if ev_eng is SC:
                        ev_eng.op(nc.scalar.copy(
                            ln2T[:, 8 * fg:8 * fg + 8, 128 * t:128 * (t + 1)],
                            pst[:, :, :]))
                    else:
                        ev_eng.op(nc.vector.tensor_copy(
                            ln2T[:, 8 * fg:8 * fg + 8, 128 * t:128 * (t + 1)],
                            pst[:, :, :]))
                    tok_ev3[rr] = (ev_eng, ev_eng.mark())
                tok_lnb_user = PE.n
            barrier()
            pb.close()

            ps_b = p3.enter_context(nc.psum_tensor("ps_b", [128, NT, 512], F32))
            wb_bf = sb3("wb_bf", [128, 2, 2, 512], BF16)
            brow = sb3("brow", [1, 2, 512], BF16)
            tok_wb = [None, None]
            tok_wb_user = [0, 0]
            tok_br = [None, None]
            tok_br_user = [0, 0]
            tok_psb_user = [0] * NT
            for os_ in range(8):
                bsl = os_ % 2
                if tok_br_user[bsl]:
                    gB.qh.wait_ge(PE.sem, tok_br_user[bsl])
                tok_br[bsl] = gB.start(bsl, brow[:, bsl, :],
                                       biasb[:, 512 * os_:512 * (os_ + 1)])
                for fg in range(16):
                    sl = (os_ * 16 + fg) % 2
                    if tok_wb_user[sl]:
                        gWO.qh.wait_ge(PE.sem, tok_wb_user[sl])
                    if os_ == 0 and fg == 0:
                        gWO.qh.wait_ge(gPC.sems[tk_pc_wb[0]], tk_pc_wb[1])
                    tok_wb[sl] = gWO.start(sl, wb_bf[:, sl, :, :],
                                           wbt_bf_d[256 * fg:256 * (fg + 1),
                                                    512 * os_:512 * (os_ + 1)]
                                           .rearrange("(a p) n -> p a n", p=128))
                    gWO.wait(PE, tok_wb[sl])
                    for ff in range(2):
                        f = 2 * fg + ff
                        for t in range(NT):
                            st = (f == 0)
                            if st and tok_psb_user[t]:
                                PE.wait(VE, tok_psb_user[t])
                            PE.op(nc.tensor.matmul(ps_b[:, t, :],
                                                   ln2T[:, f, 128 * t:128 * (t + 1)],
                                                   wb_bf[:, sl, ff, :],
                                                   start=st, stop=False))
                    tok_wb_user[sl] = PE.mark()
                gB.wait(PE, tok_br[bsl])
                for t in range(NT):
                    PE.op(nc.tensor.matmul(ps_b[:, t, :], onesb[:], brow[:, bsl, :],
                                           start=False, stop=True))
                t_bd = PE.mark()
                tok_br_user[bsl] = t_bd
                for t in range(NT):
                    VE.wait(PE, t_bd)
                    VE.op(nc.vector.tensor_tensor(
                        x_res[:, t, 512 * os_:512 * (os_ + 1)],
                        x_res[:, t, 512 * os_:512 * (os_ + 1)],
                        ps_b[:, t, :], ALU.add))
                    tok_psb_user[t] = VE.mark()
            barrier()
            p3.close()
            if DEBUG:
                SP.h.wait_ge(VE.sem, VE.n)
                dST.start(0, taps["t_x3"][:], x_res[:])
                tap_quiesce()

        st_toks = []
        if BUILD_PHASES >= 6:
            st_toks = emit_ffn(store_out=True)

        for s_, c_ in zip(dST.sems, dST.cnt):
            if c_:
                SP.h.wait_ge(s_, c_)
        if DEBUG:
            for s_, c_ in zip(gWO.sems, gWO.cnt):
                if c_:
                    SP.h.wait_ge(s_, c_)
        barrier()
        for st_ in (p1a, pa, p1):
            st_.close()

    @blk.gpsimd
    def _(_g):
        emit()

    top.close()
    return nc


def kernel(L0_attn_a_hidden_states, L0_attn_a_Wqkv, L0_attn_a_Wo,
           L0_attn_b_weight, L0_attn_b_bias,
           L0_ffn_W_res, L0_ffn_W_pre, L0_ffn_W_post,
           L0_ffn_H_res_base, L0_ffn_H_pre_base, L0_ffn_H_post_base,
           L0_ffn_gate, L0_ffn_W_branch):
    x = np.asarray(L0_attn_a_hidden_states, dtype=np.float32)
    wqkv = np.ascontiguousarray(np.asarray(L0_attn_a_Wqkv, dtype=np.float32))
    wo = np.ascontiguousarray(np.asarray(L0_attn_a_Wo, dtype=np.float32))
    wbt = np.ascontiguousarray(np.asarray(L0_attn_b_weight, dtype=np.float32).T)
    biasb = np.ascontiguousarray(np.asarray(L0_attn_b_bias, dtype=np.float32)[None, :])
    wrppt = np.ascontiguousarray(np.concatenate([
        np.asarray(L0_ffn_W_res, dtype=np.float32),
        np.asarray(L0_ffn_W_pre, dtype=np.float32),
        np.asarray(L0_ffn_W_post, dtype=np.float32)], axis=0).T)
    basespp = np.ascontiguousarray(np.broadcast_to(np.concatenate([
        np.asarray(L0_ffn_H_res_base, dtype=np.float32).reshape(-1),
        np.asarray(L0_ffn_H_pre_base, dtype=np.float32),
        np.asarray(L0_ffn_H_post_base, dtype=np.float32)]), (128, 24)))
    gate = np.asarray(L0_ffn_gate, dtype=np.float32).reshape(1, 1)
    wbr = np.ascontiguousarray(np.asarray(L0_ffn_W_branch, dtype=np.float32))

    cols = np.arange(WD)
    prow = np.arange(128)
    in_maps = []
    for c in range(8):
        b, j = c // 4, c % 4
        chunks = [j + 4 * i for i in range(NT)]
        xq_ = np.ascontiguousarray(
            np.concatenate([x[b, 128 * qc:128 * (qc + 1)] for qc in chunks], axis=0))
        Dm = (prow[:, None] - (cols[None, :] - C0) - 128 * j).astype(np.float32)
        dm = np.where(Dm <= 0, Dm, -60000.0).astype(np.float16).astype(np.float32)
        in_maps.append({
            "xk": np.ascontiguousarray(x[b]),
            "xq": xq_,
            "wqkv": wqkv, "wo": wo, "wbt": wbt, "biasb": biasb,
            "wrppt": wrppt, "basespp": basespp, "gate": gate, "wbr": wbr,
            "dmask": dm,
        })

    nc = build_program()
    res = run_bass_kernel_spmd(nc, in_maps, core_ids=list(range(8)))

    out = np.empty((2, S, D), dtype=np.float32)
    for c in range(8):
        b, j = c // 4, c % 4
        yc = np.asarray(res.results[c]["y"])
        for i in range(NT):
            qc = j + 4 * i
            out[b, 128 * qc:128 * (qc + 1)] = yc[128 * i:128 * (i + 1)]
    if DEBUG:
        kernel.last_results = res
    return out
